# version 01
# baseline (speedup 1.0000x reference)
import contextlib
import numpy as np
import ml_dtypes
import concourse.bass as bass
import concourse.mybir as mybir
from concourse.bass_utils import run_bass_kernel_spmd

F32 = mybir.dt.float32
BF16 = mybir.dt.bfloat16
I32 = mybir.dt.int32
AF = mybir.ActivationFunctionType
AX = mybir.AxisListType
ALU = mybir.AluOpType
BF = ml_dtypes.bfloat16

T = 2048
TT = 512
NTT = T // TT
D = 1024
NC = D // 128


ENGS = ("tensor", "vector", "scalar", "gpsimd", "sync")


class Tok:
    __slots__ = ("eng", "val")

    def __init__(self, eng, val):
        self.eng = eng
        self.val = val


class Buf:
    __slots__ = ("w", "r", "name")

    def __init__(self, name=""):
        self.w = {}
        self.r = {}
        self.name = name


class Em:
    def __init__(self, nc):
        self.nc = nc
        self.recs = {e: [] for e in ENGS}
        self.cnt = {e: 0 for e in ENGS}
        self.wm = {e: {e2: 0 for e2 in ENGS} for e in ENGS}
        self.nbuf = 0

    def buf(self, name=""):
        self.nbuf += 1
        return Buf(name or f"b{self.nbuf}")

    def fence(self):
        self.fence_counts = dict(self.cnt)

    def op(self, eng, fn, reads=(), writes=(), deps=(), dma=False):
        need = {}

        def add(e2, v):
            need[e2] = max(need.get(e2, 0), v)

        for b in reads:
            for e2, v in b.w.items():
                add(e2, v)
        for b in writes:
            for e2, v in b.w.items():
                add(e2, v)
            for e2, v in b.r.items():
                add(e2, v)
        for t in deps:
            add(t.eng, t.val)
        for e2, v in getattr(self, "fence_counts", {}).items():
            add(e2, v)
        waits = []
        for e2, v in sorted(need.items()):
            if eng == "tensor" and e2 == "tensor":
                continue
            if v > self.wm[eng][e2]:
                waits.append((e2, v))
                self.wm[eng][e2] = v
        inc = 16 if dma else 1
        self.cnt[eng] += inc
        tok = Tok(eng, self.cnt[eng])
        self.recs[eng].append((waits, fn, inc))
        for b in reads:
            b.r[eng] = self.cnt[eng]
        for b in writes:
            b.w = {eng: self.cnt[eng]}
            b.r = {}
        return tok

    def dma(self, queue, out_ap, in_ap, reads=(), writes=(), deps=(), **kw):
        return self.op(queue, lambda e, o=out_ap, i=in_ap: e.dma_start(out=o, in_=i, **kw),
                       reads=reads, writes=writes, deps=deps, dma=True)

    def finalize(self):
        nc = self.nc
        with contextlib.ExitStack() as stack:
            sems = {e: stack.enter_context(nc.semaphore(f"sem_{e}")) for e in ENGS}
            mark = self.mark_fn
            with nc.Block() as block:
                def mk(eng_name):
                    def body(eng):
                        for waits, fn, inc in self.recs[eng_name]:
                            for (e2, v) in waits:
                                eng.wait_ge(sems[e2], v)
                            fn(eng).then_inc(sems[eng_name], inc)
                        if eng_name == "sync":
                            for e2 in ENGS:
                                if self.cnt[e2] > self.wm["sync"][e2]:
                                    eng.wait_ge(sems[e2], self.cnt[e2])
                            mark(eng).then_inc(sems["sync"], 16)
                    return body
                block.tensor(mk("tensor"))
                block.vector(mk("vector"))
                block.scalar(mk("scalar"))
                block.gpsimd(mk("gpsimd"))
                block.sync(mk("sync"))
            nc.gpsimd.wait_ge(sems["sync"], self.cnt["sync"] + 16)
            for e2 in ENGS:
                nc.gpsimd.sem_clear(sems[e2])



def fm(x_tok_feat):
    t, d = x_tok_feat.shape
    return np.ascontiguousarray(x_tok_feat.T.reshape(d // 128, 128, t))


def unfm(x_fm):
    c, p, t = x_fm.shape
    return np.ascontiguousarray(x_fm.reshape(c * p, t).T)


def shard_x(x):
    return [fm(x[c // 4, (c % 4) * T:(c % 4 + 1) * T, :]) for c in range(8)]


def unshard_x(shards):
    x = np.empty((2, 8192, D), np.float32)
    for c in range(8):
        x[c // 4, (c % 4) * T:(c % 4 + 1) * T, :] = unfm(shards[c])
    return x


def chunk_vec(v):
    n = v.shape[0]
    return np.ascontiguousarray(v.reshape(n // 128, 128).T)


CONSTW = 258


def host_consts():
    c = np.zeros((128, CONSTW), np.float32)
    c[:, 128] = 1.0
    c[0, 0:128] = 1.0
    c[:, 130:258] = np.eye(128, dtype=np.float32)
    return c



class Ctx:
    def __init__(self, nc, stack):
        self.nc = nc
        self.em = Em(nc)
        self.stack = stack
        self.tiles = {}
        self.bufs = {}
        self.psum = []

    def sbuf(self, name, shape, dtype, nbuf=1):
        t = self.stack.enter_context(self.nc.sbuf_tensor(name, shape, dtype))
        self.tiles[name] = t
        self.bufs[name] = self.em.buf(name) if nbuf == 1 else \
            [self.em.buf(f"{name}{i}") for i in range(nbuf)]
        return t

    def alloc_psum(self, n_f32, n_bf16=0):
        for i in range(n_f32):
            t = self.stack.enter_context(self.nc.psum_tensor(f"pf{i}", [128, 512], F32))
            self.psum.append((t, self.em.buf(f"pf{i}")))
        self.psum_bf = []
        for i in range(n_bf16):
            t = self.stack.enter_context(self.nc.psum_tensor(f"pb{i}", [128, 1024], BF16))
            self.psum_bf.append((t, self.em.buf(f"pb{i}")))
        self.ln_banks = (4, 5, 6, 7) if n_f32 >= 8 else (3, 4, 5, 6)

    def load_consts(self, consts_ap):
        cst = self.sbuf("cst", [128, CONSTW], F32)
        self.em.dma("sync", cst[:, :], consts_ap, writes=[self.bufs["cst"]])
        self.ones_col = cst[:, 128:129]
        self.ones_row = cst[0:1, 0:128]
        self.ident = cst[:, 130:258]
        self.cb = self.bufs["cst"]
        identb_t = self.sbuf("identb_t", [128, 128], BF16)
        self.em.op("vector", lambda e, o=identb_t[:, :], i=self.ident:
                   e.tensor_copy(o, i), reads=[self.cb], writes=[self.bufs["identb_t"]])
        self.identb = identb_t[:, :]
        self.em.mark_fn = lambda eng: eng.dma_start(out=cst[0:1, 129:130],
                                                    in_=consts_ap[0:1, 129:130])



LN_TTS = None
LN_FENCE = False
LN_DBG_TILE = -1


def emit_ln(ctx, x_tile, x_buf, out_f32, out_f32_buf, out_bf, out_bf_buf,
            gamma=None, beta=None, eps=1e-5, x_scale=1.0, gb_buf=None):
    nc, em = ctx.nc, ctx.em
    sq = ctx.tiles["sq"]; sq_bufs = ctx.bufs["sq"]
    row = ctx.tiles["row"]; row_buf = ctx.bufs["row"]
    s1 = ctx.psum[4]; s2 = ctx.psum[5]; ba = ctx.psum[6]; bb = ctx.psum[7]
    cb = ctx.cb

    for tt in (range(NTT) if LN_TTS is None else LN_TTS):
        t0 = tt * TT
        for c in range(NC):
            xa = x_tile[:, c * T + t0: c * T + t0 + TT]
            sqa = sq[:, (c % 2) * TT:(c % 2) * TT + TT]
            em.op("scalar", lambda e, i=xa, o=sqa: e.activation(o, i, AF.Square),
                  reads=[x_buf], writes=[sq_bufs[c % 2]])
            em.op("tensor", lambda e, o=s2[0][0:1, :], w=ctx.ones_col, r=sqa, st=(c == 0),
                  sp=(c == NC - 1): e.matmul(o, w, r, start=st, stop=sp),
                  reads=[sq_bufs[c % 2], cb], writes=[s2[1]])

        def mm_s1(e, t0=t0):
            last = None
            for c in range(NC):
                last = e.matmul(s1[0][0:1, :], ctx.ones_col,
                                x_tile[:, c * T + t0: c * T + t0 + TT],
                                start=(c == 0), stop=(c == NC - 1))
            return last
        em.op("tensor", mm_s1, reads=[x_buf, cb], writes=[s1[1]])

        r_mean = row[0:1, 0 * TT:1 * TT]
        r_rstd = row[0:1, 1 * TT:2 * TT]
        r_m2 = row[0:1, 2 * TT:3 * TT]
        r_b = row[0:1, 2 * TT:3 * TT]
        sc = x_scale
        em.op("vector", lambda e, o=r_mean, i=s1[0][0:1, :]:
              e.tensor_scalar_mul(o, i, sc / D), reads=[s1[1]], writes=[row_buf])
        em.op("vector", lambda e, o=r_m2, i=r_mean: e.tensor_mul(o, i, i),
              reads=[row_buf], writes=[row_buf])
        em.op("vector", lambda e, o=r_rstd, i0=s2[0][0:1, :], i1=r_m2:
              e.scalar_tensor_tensor(o, i0, sc * sc / D, i1, ALU.mult, ALU.subtract),
              reads=[s2[1], row_buf], writes=[row_buf])
        em.op("vector", lambda e, o=r_rstd, i=r_rstd: e.tensor_scalar_add(o, i, eps),
              reads=[row_buf], writes=[row_buf])
        em.op("scalar", lambda e, o=r_rstd, i=r_rstd: e.activation(o, i, AF.Sqrt),
              reads=[row_buf], writes=[row_buf])
        em.op("vector", lambda e, o=r_rstd, i=r_rstd: e.reciprocal(o, i),
              reads=[row_buf], writes=[row_buf])
        em.op("vector", lambda e, o=r_b, i0=r_mean, i1=r_rstd:
              e.scalar_tensor_tensor(o, i0, -1.0, i1, ALU.mult, ALU.mult),
              reads=[row_buf], writes=[row_buf])
        if sc != 1.0:
            em.op("vector", lambda e, o=r_rstd, i=r_rstd: e.tensor_scalar_mul(o, i, sc),
                  reads=[row_buf], writes=[row_buf])
        em.op("tensor", lambda e, o=ba[0][:, :], r=r_rstd:
              e.matmul(o, ctx.ones_row, r, start=True, stop=True),
              reads=[row_buf, cb], writes=[ba[1]])
        em.op("tensor", lambda e, o=bb[0][:, :], r=r_b:
              e.matmul(o, ctx.ones_row, r, start=True, stop=True),
              reads=[row_buf, cb], writes=[bb[1]])
        if LN_DBG_TILE == tt:
            dbg = ctx.tiles["lndbg"]
            dbgb = ctx.bufs["lndbg"]
            em.op("vector", lambda e, o=dbg[0:1, 0:4 * TT], i=row[0:1, :]:
                  e.tensor_copy(o, i), reads=[row_buf], writes=[dbgb])
            em.op("vector", lambda e, o=dbg[:, 4 * TT:5 * TT], i=ba[0][:, :]:
                  e.tensor_copy(o, i), reads=[ba[1]], writes=[dbgb])
            em.op("vector", lambda e, o=dbg[:, 5 * TT:6 * TT], i=bb[0][:, :]:
                  e.tensor_copy(o, i), reads=[bb[1]], writes=[dbgb])
        for c in range(NC):
            xa = x_tile[:, c * T + t0: c * T + t0 + TT]
            tmp = sq[:, (c % 2) * TT:(c % 2) * TT + TT]
            em.op("vector", lambda e, o=tmp, i0=xa, i1=ba[0][:, :]:
                  e.tensor_mul(o, i0, i1), reads=[x_buf, ba[1]], writes=[sq_bufs[c % 2]])
            if gamma is None:
                dsts = []
                if out_f32 is not None:
                    dsts.append((out_f32[:, c * T + t0: c * T + t0 + TT], out_f32_buf))
                if out_bf is not None and out_f32 is None:
                    dsts.append((out_bf[:, c * T + t0: c * T + t0 + TT], out_bf_buf))
                dst, dbuf = dsts[0]
                em.op("vector", lambda e, o=dst, i0=tmp, i1=bb[0][:, :]:
                      e.tensor_add(o, i0, i1), reads=[sq_bufs[c % 2], bb[1]], writes=[dbuf])
                if out_bf is not None and out_f32 is not None:
                    dstb = out_bf[:, c * T + t0: c * T + t0 + TT]
                    em.op("vector", lambda e, o=dstb, i=dst: e.tensor_copy(o, i),
                          reads=[out_f32_buf], writes=[out_bf_buf])
            else:
                em.op("vector", lambda e, o=tmp, i0=tmp, i1=bb[0][:, :]:
                      e.tensor_add(o, i0, i1), reads=[sq_bufs[c % 2], bb[1]],
                      writes=[sq_bufs[c % 2]])
                dstb = out_bf[:, c * T + t0: c * T + t0 + TT]
                em.op("vector", lambda e, o=dstb, i=tmp, g=gamma(c), b_=beta(c):
                      e.tensor_scalar(o, i, g, b_, ALU.mult, ALU.add),
                      reads=[sq_bufs[c % 2], cb] + ([gb_buf] if gb_buf else []),
                      writes=[out_bf_buf])
        if LN_FENCE:
            em.fence()


class GemmState:
    def __init__(self):
        self.rot = 0


def emit_gemm(ctx, *, x_tile, x_buf, x_col0, Kdim, w_dram_ap, w_buf, Ndim,
              evac, slab_n=512, tts=None, load_queue="sync",
              psum_ids=(0, 1), tile_T=T, gst=None, w_f32=False, cast_eng="scalar",
              kc_off=None):
    em = ctx.em
    nkc = Kdim // 128
    wt = [ctx.tiles["wsA"], ctx.tiles["wsB"]]
    wtb = [ctx.bufs["wsA"], ctx.bufs["wsB"]]
    if w_f32:
        wg = [ctx.tiles["wgA"], ctx.tiles["wgB"]]
        wgb = [ctx.bufs["wgA"], ctx.bufs["wgB"]]
    if tts is None:
        tts = range(tile_T // TT)
    if gst is None:
        gst = GemmState()
    nslab = (Ndim + slab_n - 1) // slab_n
    for ns in range(nslab):
        n0 = ns * slab_n
        nsz = min(slab_n, Ndim - n0)
        w, wb = wt[ns % 2], wtb[ns % 2]
        src = w_dram_ap[:, n0:n0 + nsz].rearrange("(c p) n -> p c n", p=128)
        if w_f32:
            g, gb = wg[ns % 2], wgb[ns % 2]
            gdst = g[:, 0:nkc * nsz].rearrange("p (c n) -> p c n", n=nsz)
            em.dma(load_queue, gdst, src, reads=[w_buf], writes=[gb])
            em.op(cast_eng,
                  (lambda e, o=w[:, 0:nkc * nsz], i=g[:, 0:nkc * nsz]:
                   e.copy(o, i)) if cast_eng == "scalar" else
                  (lambda e, o=w[:, 0:nkc * nsz], i=g[:, 0:nkc * nsz]:
                   e.tensor_copy(o, i)),
                  reads=[gb], writes=[wb])
        else:
            dst = w[:, 0:nkc * nsz].rearrange("p (c n) -> p c n", n=nsz)
            em.dma(load_queue, dst, src, reads=[w_buf], writes=[wb])
        for tt in tts:
            t0 = tt * TT
            for oc in range((nsz + 127) // 128):
                osz = min(128, nsz - oc * 128)
                pt, pb = ctx.psum[psum_ids[gst.rot % len(psum_ids)]]
                gst.rot += 1

                def mm(e, w=w, oc=oc, osz=osz, t0=t0, pt=pt, nsz=nsz):
                    last = None
                    for kc in range(nkc):
                        ko = 0 if kc_off is None else kc_off[kc]
                        last = e.matmul(pt[0:osz, 0:TT],
                                        w[:, kc * nsz + oc * 128: kc * nsz + oc * 128 + osz],
                                        x_tile[:, x_col0 + ko + kc * tile_T + t0:
                                               x_col0 + ko + kc * tile_T + t0 + TT],
                                        start=(kc == 0), stop=(kc == nkc - 1))
                    return last
                em.op("tensor", mm, reads=[wb, x_buf], writes=[pb])
                evac(n0 + oc * 128, tt, pt, pb)



DEBUG_L1 = False


def build_l1_ffn():
    nc = bass.Bass("TRN2", target_bir_lowering=False, debug=False)
    xin = nc.dram_tensor("xin", [NC, 128, T], F32, kind="ExternalInput").ap()
    fc1q = nc.dram_tensor("fc1q", [D, 4096], I32, kind="ExternalInput").ap()
    fc1s = nc.dram_tensor("fc1s", [8, 4096], F32, kind="ExternalInput").ap()
    fc1b = nc.dram_tensor("fc1b", [128, 32], F32, kind="ExternalInput").ap()
    fc2q = nc.dram_tensor("fc2q", [4096, D], I32, kind="ExternalInput").ap()
    fc2s = nc.dram_tensor("fc2s", [32, D], F32, kind="ExternalInput").ap()
    fc2b = nc.dram_tensor("fc2b", [128, 8], F32, kind="ExternalInput").ap()
    lng = nc.dram_tensor("lng", [128, 8], F32, kind="ExternalInput").ap()
    lnb = nc.dram_tensor("lnb", [128, 8], F32, kind="ExternalInput").ap()
    consts = nc.dram_tensor("consts", [128, CONSTW], F32, kind="ExternalInput").ap()
    yout = nc.dram_tensor("yout", [NC, 128, T], F32, kind="ExternalOutput").ap()
    w1d = nc.dram_tensor("w1d", [D, 4096], BF16).ap()
    w2d = nc.dram_tensor("w2d", [4096, D], BF16).ap()

    with contextlib.ExitStack() as stack:
        ctx = Ctx(nc, stack)
        em = ctx.em
        xr = ctx.sbuf("xr", [128, NC * T], F32)
        xn = ctx.sbuf("xn", [128, NC * T], BF16)
        yt = ctx.sbuf("yt", [128, 32 * TT], BF16) if DEBUG_L1 != "ln" else None
        ctx.sbuf("wsA", [128, 32 * 256], BF16)
        ctx.sbuf("wsB", [128, 32 * 256], BF16)
        wq = ctx.sbuf("wq", [128, 2 * 512], I32, nbuf=2)
        wqf = ctx.sbuf("wqf", [128, 2 * 512], F32, nbuf=2)
        srow = ctx.sbuf("srow", [1, 2 * 512], F32, nbuf=2)
        sq = ctx.sbuf("sq", [128, 2 * TT], F32, nbuf=2)
        row = ctx.sbuf("row", [1, 3 * TT], F32)
        bia = ctx.sbuf("bia", [128, 64], F32)
        if DEBUG_L1 == "ln" and LN_DBG_TILE >= 0:
            ctx.sbuf("lndbg", [128, 6 * TT], F32)
        ctx.alloc_psum(8)
        ctx.load_consts(consts)
        cb = ctx.cb

        bb = em.buf("bia")
        em.dma("sync", bia[:, 0:32], fc1b, writes=[bb])
        em.dma("sync", bia[:, 32:40], fc2b, writes=[bb])
        em.dma("sync", bia[:, 40:48], lng, writes=[bb])
        em.dma("sync", bia[:, 48:56], lnb, writes=[bb])
        em.op("vector", lambda e: e.tensor_scalar_mul(bia[:, 56:64], bia[:, 32:40], 0.5),
              reads=[bb], writes=[bb])

        xr_buf = ctx.bufs["xr"]
        for c in range(NC):
            em.dma("sync", xr[:, c * T:(c + 1) * T], xin[c], writes=[xr_buf])

        w1_buf = em.buf("w1d")
        w2_buf = em.buf("w2d")

        def dequant(qap, sap, Kdim, Ndim, out_ap, out_buf):
            for kc in range(Kdim // 128):
                for ns in range(Ndim // 512):
                    par = (kc * (Ndim // 512) + ns) % 2
                    n0 = ns * 512
                    qsl = wq[:, par * 512:(par + 1) * 512]
                    qfl = wqf[:, par * 512:(par + 1) * 512]
                    ssl = srow[0:1, par * 512:(par + 1) * 512]
                    em.dma("sync", qsl, qap[kc * 128:(kc + 1) * 128, n0:n0 + 512],
                           writes=[ctx.bufs["wq"][par]])
                    em.dma("sync", ssl, sap[kc:kc + 1, n0:n0 + 512],
                           writes=[ctx.bufs["srow"][par]])
                    em.op("scalar", lambda e, o=qfl, i=qsl: e.copy(o, i),
                          reads=[ctx.bufs["wq"][par]], writes=[ctx.bufs["wqf"][par]])
                    pt, pb = ctx.psum[2 + par]
                    em.op("tensor", lambda e, o=pt[:, :], r=ssl:
                          e.matmul(o, ctx.ones_row, r, start=True, stop=True),
                          reads=[ctx.bufs["srow"][par], cb], writes=[pb])
                    stg = ctx.tiles["wsA"][:, par * 512:(par + 1) * 512]
                    sb = ctx.bufs["wsA"]
                    em.op("vector", lambda e, o=stg, i0=qfl, i1=pt[:, :]:
                          e.scalar_tensor_tensor(o, i0, -128.0, i1, ALU.add, ALU.mult),
                          reads=[ctx.bufs["wqf"][par], pb], writes=[sb])
                    em.dma("gpsimd", out_ap[kc * 128:(kc + 1) * 128, n0:n0 + 512], stg,
                           reads=[sb], writes=[out_buf])

        if DEBUG_L1 != "ln":
            dequant(fc1q, fc1s, D, 4096, w1d, w1_buf)
            dequant(fc2q, fc2s, 4096, D, w2d, w2_buf)

        emit_ln(ctx, xr, xr_buf, None, None, xn, ctx.bufs["xn"],
                gamma=lambda c: bia[:, 40 + c:41 + c],
                beta=lambda c: bia[:, 48 + c:49 + c], gb_buf=bb)

        if DEBUG_L1 == "ln":
            if LN_DBG_TILE >= 0:
                dl = nc.dram_tensor("dbg_ln", [128, 6 * TT], F32, kind="ExternalOutput").ap()
                em.dma("gpsimd", dl, ctx.tiles["lndbg"][:, :], reads=[ctx.bufs["lndbg"]])
            dxn = nc.dram_tensor("dbg_xn", [128, NC * T], BF16, kind="ExternalOutput").ap()
            em.dma("gpsimd", dxn, xn[:, :], reads=[ctx.bufs["xn"]])
            yo0 = ctx.sbuf("yo", [128, NC * TT], F32)
            _ln_out_tile(ctx, xr, xr_buf, yo0, ctx.bufs["yo"], yout)
            em.finalize()
            return nc

        for c in range(NC):
            em.op("vector", lambda e, o=xr[:, c * T:(c + 1) * T]:
                  e.tensor_scalar_mul(o, o, 2.0), reads=[xr_buf], writes=[xr_buf])

        yt_buf = ctx.bufs["yt"]
        gst = GemmState()
        for tt in range(NTT):
            def evac1(nout0, tt_, pt, pb):
                oc = nout0 // 128
                em.op("scalar", lambda e, o=yt[:, oc * TT:oc * TT + TT], i=pt[0:128, 0:TT],
                      b=bia[:, oc:oc + 1]: e.activation(o, i, AF.Silu, bias=b),
                      reads=[pb, bb], writes=[yt_buf])
            emit_gemm(ctx, x_tile=xn, x_buf=ctx.bufs["xn"], x_col0=0, Kdim=D,
                      w_dram_ap=w1d, w_buf=w1_buf, Ndim=4096, evac=evac1,
                      slab_n=512, tts=[tt], psum_ids=(0, 1), gst=gst)

            def evac2(nout0, tt_, pt, pb, tt_out=tt):
                oc = nout0 // 128
                dst = xr[:, oc * T + tt_out * TT: oc * T + tt_out * TT + TT]
                em.op("vector", lambda e, o=dst, i0=pt[0:128, 0:TT], i1=dst:
                      e.scalar_tensor_tensor(o, i0, 0.5, i1, ALU.mult, ALU.add),
                      reads=[pb, xr_buf], writes=[xr_buf])
                em.op("vector", lambda e, o=dst, b=bia[:, 56 + oc:57 + oc]:
                      e.tensor_scalar(o, o, b, None, ALU.add),
                      reads=[xr_buf, bb], writes=[xr_buf])
            emit_gemm(ctx, x_tile=yt, x_buf=yt_buf, x_col0=0, Kdim=4096,
                      w_dram_ap=w2d, w_buf=w2_buf, Ndim=D, evac=evac2,
                      slab_n=256, tts=[0], psum_ids=(0, 1), tile_T=TT, gst=gst)

        if DEBUG_L1:
            dxn = nc.dram_tensor("dbg_xn", [128, NC * T], BF16, kind="ExternalOutput").ap()
            em.dma("gpsimd", dxn, xn[:, :], reads=[ctx.bufs["xn"]])
            dyt = nc.dram_tensor("dbg_yt", [128, 32 * TT], BF16, kind="ExternalOutput").ap()
            em.dma("gpsimd", dyt, yt[:, :], reads=[yt_buf])
            dr2 = nc.dram_tensor("dbg_r2", [128, NC * T], F32, kind="ExternalOutput").ap()
            em.dma("gpsimd", dr2, xr[:, :], reads=[xr_buf])

        yo = ctx.sbuf("yo", [128, NC * TT], F32)
        _ln_out_tile(ctx, xr, xr_buf, yo, ctx.bufs["yo"], yout)

        em.finalize()
    return nc


def _ln_out_tile(ctx, x_tile, x_buf, yo, yo_buf, yout_ap, eps=1e-5):
    em = ctx.em
    sq = ctx.tiles["sq"]; sq_bufs = ctx.bufs["sq"]
    row = ctx.tiles["row"]; row_buf = ctx.bufs["row"]
    s1 = ctx.psum[4]; s2 = ctx.psum[5]; ba = ctx.psum[6]; bb_ = ctx.psum[7]
    cb = ctx.cb
    for tt in range(NTT):
        t0 = tt * TT
        for c in range(NC):
            xa = x_tile[:, c * T + t0: c * T + t0 + TT]
            sqa = sq[:, (c % 2) * TT:(c % 2) * TT + TT]
            em.op("scalar", lambda e, i=xa, o=sqa: e.activation(o, i, AF.Square),
                  reads=[x_buf], writes=[sq_bufs[c % 2]])
            em.op("tensor", lambda e, o=s2[0][0:1, :], r=sqa, st=(c == 0), sp=(c == NC - 1):
                  e.matmul(o, ctx.ones_col, r, start=st, stop=sp),
                  reads=[sq_bufs[c % 2], cb], writes=[s2[1]])

        def mm_s1(e, t0=t0):
            last = None
            for c in range(NC):
                last = e.matmul(s1[0][0:1, :], ctx.ones_col,
                                x_tile[:, c * T + t0: c * T + t0 + TT],
                                start=(c == 0), stop=(c == NC - 1))
            return last
        em.op("tensor", mm_s1, reads=[x_buf, cb], writes=[s1[1]])
        r_mean = row[0:1, 0 * TT:1 * TT]; r_rstd = row[0:1, 1 * TT:2 * TT]
        r_m2 = row[0:1, 2 * TT:3 * TT]; r_b = row[0:1, 2 * TT:3 * TT]
        em.op("vector", lambda e, o=r_mean, i=s1[0][0:1, :]:
              e.tensor_scalar_mul(o, i, 1.0 / D), reads=[s1[1]], writes=[row_buf])
        em.op("vector", lambda e, o=r_m2, i=r_mean: e.tensor_mul(o, i, i),
              reads=[row_buf], writes=[row_buf])
        em.op("vector", lambda e, o=r_rstd, i0=s2[0][0:1, :], i1=r_m2:
              e.scalar_tensor_tensor(o, i0, 1.0 / D, i1, ALU.mult, ALU.subtract),
              reads=[s2[1], row_buf], writes=[row_buf])
        em.op("vector", lambda e, o=r_rstd, i=r_rstd: e.tensor_scalar_add(o, i, eps),
              reads=[row_buf], writes=[row_buf])
        em.op("scalar", lambda e, o=r_rstd, i=r_rstd: e.activation(o, i, AF.Sqrt),
              reads=[row_buf], writes=[row_buf])
        em.op("vector", lambda e, o=r_rstd, i=r_rstd: e.reciprocal(o, i),
              reads=[row_buf], writes=[row_buf])
        em.op("vector", lambda e, o=r_b, i0=r_mean, i1=r_rstd:
              e.scalar_tensor_tensor(o, i0, -1.0, i1, ALU.mult, ALU.mult),
              reads=[row_buf], writes=[row_buf])
        em.op("tensor", lambda e, o=ba[0][:, :], r=r_rstd:
              e.matmul(o, ctx.ones_row, r, start=True, stop=True),
              reads=[row_buf, cb], writes=[ba[1]])
        em.op("tensor", lambda e, o=bb_[0][:, :], r=r_b:
              e.matmul(o, ctx.ones_row, r, start=True, stop=True),
              reads=[row_buf, cb], writes=[bb_[1]])
        for c in range(NC):
            xa = x_tile[:, c * T + t0: c * T + t0 + TT]
            tmp = sq[:, (c % 2) * TT:(c % 2) * TT + TT]
            dst = yo[:, c * TT:(c + 1) * TT]
            em.op("vector", lambda e, o=tmp, i0=xa, i1=ba[0][:, :]:
                  e.tensor_mul(o, i0, i1), reads=[x_buf, ba[1]], writes=[sq_bufs[c % 2]])
            em.op("vector", lambda e, o=dst, i0=tmp, i1=bb_[0][:, :]:
                  e.tensor_add(o, i0, i1), reads=[sq_bufs[c % 2], bb_[1]], writes=[yo_buf])
            em.dma("gpsimd", yout_ap[c, :, t0:t0 + TT], dst, reads=[yo_buf])





def emit_gate(ctx, xb, xb_buf, gw_ap, gb_rep=None, gb_buf=None, n_exp=8):
    nc, em = ctx.nc, ctx.em
    gwb = ctx.sbuf("gwb", [128, NC * n_exp], BF16)
    gwf = ctx.sbuf("gwf", [128, NC * n_exp], F32)
    gt = ctx.sbuf("gt", [128, 6 * n_exp + 4], F32, nbuf=1)
    w_all = ctx.sbuf("w_all", [128, (T // 128) * n_exp], F32)
    wT_sb = ctx.sbuf("wT_sb", [n_exp, T], F32)
    wrdr = nc.dram_tensor("wrdr", [n_exp, T], F32).ap()
    wrdr_buf = em.buf("wrdr")
    cb = ctx.cb

    src = gw_ap.rearrange("(c p) e -> p c e", p=128)
    em.dma("sync", gwf[:, :].rearrange("p (c e) -> p c e", e=n_exp), src,
           writes=[ctx.bufs["gwf"]])
    em.op("vector", lambda e, o=gwb[:, :], i=gwf[:, :]: e.tensor_copy(o, i),
          reads=[ctx.bufs["gwf"]], writes=[ctx.bufs["gwb"]])

    gp, gpb = ctx.psum[6]
    tp, tpb = ctx.psum[7]
    gtb = ctx.bufs["gt"]
    wab = ctx.bufs["w_all"]
    wTb = ctx.bufs["wT_sb"]
    for t in range(T // 128):
        def mm(e, t=t):
            last = None
            for c in range(NC):
                last = e.matmul(gp[0:128, 0:n_exp],
                                xb[:, c * T + t * 128: c * T + t * 128 + 128],
                                gwb[:, c * n_exp:(c + 1) * n_exp],
                                start=(c == 0), stop=(c == NC - 1))
            return last
        em.op("tensor", mm, reads=[xb_buf, ctx.bufs["gwb"]], writes=[gpb])
        E = n_exp
        g0 = gt[:, 0:E]; m1 = gt[:, 5*E:5*E+1]; m1n = gt[:, 5*E+1:5*E+2]
        m2 = gt[:, 5*E+2:5*E+3]; rden = gt[:, 5*E+3:5*E+4]
        msk = gt[:, E:2*E]; x2 = gt[:, 2*E:3*E]; p = gt[:, 3*E:4*E]; pk = gt[:, 4*E:5*E]
        em.op("vector", lambda e, o=g0, i=gp[0:128, 0:n_exp]: e.tensor_copy(o, i),
              reads=[gpb], writes=[gtb])
        if gb_rep is not None:
            em.op("vector", lambda e, o=g0, i=g0, b=gb_rep: e.tensor_add(o, i, b),
                  reads=[gtb, gb_buf], writes=[gtb])
        em.op("vector", lambda e, o=m1, i=g0: e.reduce_max(o, i, axis=AX.X),
              reads=[gtb], writes=[gtb])
        em.op("vector", lambda e, o=m1n, i=g0: e.reduce_max(o, i, axis=AX.X, negate=True),
              reads=[gtb], writes=[gtb])
        em.op("vector", lambda e, o=msk, i=g0, m=m1: e.tensor_single_scalar(o, i, m, ALU.is_ge),
              reads=[gtb], writes=[gtb])
        em.op("vector", lambda e, o=x2, i0=msk, i1=g0:
              e.scalar_tensor_tensor(o, i0, -1e30, i1, ALU.mult, ALU.add),
              reads=[gtb], writes=[gtb])
        em.op("vector", lambda e, o=m2, i=x2: e.reduce_max(o, i, axis=AX.X),
              reads=[gtb], writes=[gtb])
        em.op("scalar", lambda e, o=p, i=g0, b=m1n: e.activation(o, i, AF.Exp, bias=b),
              reads=[gtb], writes=[gtb])
        em.op("vector", lambda e, o=pk, i0=g0, m=m2: e.tensor_single_scalar(o, i0, m, ALU.is_ge),
              reads=[gtb], writes=[gtb])
        em.op("vector", lambda e, o=pk, i0=pk, i1=p: e.tensor_mul(o, i0, i1),
              reads=[gtb], writes=[gtb])
        em.op("vector", lambda e, o=rden, i=pk: e.reduce_sum(o, i, axis=AX.X),
              reads=[gtb], writes=[gtb])
        em.op("vector", lambda e, o=rden, i=rden: e.reciprocal(o, i),
              reads=[gtb], writes=[gtb])
        em.op("vector", lambda e, o=w_all[:, t * n_exp:(t + 1) * n_exp], i=pk, m=rden:
              e.tensor_single_scalar(o, i, m, ALU.mult),
              reads=[gtb], writes=[wab])
        em.op("tensor", lambda e, o=tp[0:n_exp, 0:128],
              i=w_all[:, t * n_exp:(t + 1) * n_exp], idt=ctx.ident:
              e.transpose(o, i, idt), reads=[wab, cb], writes=[tpb])
        em.op("vector", lambda e, o=wT_sb[0:n_exp, t * 128:(t + 1) * 128],
              i=tp[0:n_exp, 0:128]: e.tensor_copy(o, i), reads=[tpb], writes=[wTb])
    em.dma("gpsimd", wrdr, wT_sb[:, :], reads=[wTb], writes=[wrdr_buf])
    return wrdr, wrdr_buf, wT_sb, wTb


def emit_wbc(ctx, wrow, wrow_buf, e_, t0, wbc_par):
    em = ctx.em
    gp, gpb = ctx.psum[6]
    wbc = ctx.tiles["wbc"]
    wb = ctx.bufs["wbc"][wbc_par]
    dst = wbc[:, wbc_par * TT:(wbc_par + 1) * TT]
    em.op("tensor", lambda e, o=gp[:, :], r=wrow[0:1, e_ * T + t0:e_ * T + t0 + TT]:
          e.matmul(o, ctx.ones_row, r, start=True, stop=True),
          reads=[wrow_buf, ctx.cb], writes=[gpb])
    em.op("vector", lambda e, o=dst, i=gp[:, :]: e.tensor_copy(o, i),
          reads=[gpb], writes=[wb])
    return dst, wb




def emit_ln_tile(ctx, src_tile, src_buf, yout_ap, t0_out, eps=1e-5):
    em = ctx.em
    sq = ctx.tiles["sq"]; sq_bufs = ctx.bufs["sq"]
    row = ctx.tiles["row"]; row_buf = ctx.bufs["row"]
    lb = ctx.ln_banks
    s1 = ctx.psum[lb[0]]; s2 = ctx.psum[lb[1]]; ba = ctx.psum[lb[2]]; bb_ = ctx.psum[lb[3]]
    cb = ctx.cb
    for c in range(NC):
        xa = src_tile[:, c * TT:(c + 1) * TT]
        sqa = sq[:, (c % 2) * TT:(c % 2) * TT + TT]
        em.op("scalar", lambda e, i=xa, o=sqa: e.activation(o, i, AF.Square),
              reads=[src_buf], writes=[sq_bufs[c % 2]])
        em.op("tensor", lambda e, o=s2[0][0:1, :], r=sqa, st=(c == 0), sp=(c == NC - 1):
              e.matmul(o, ctx.ones_col, r, start=st, stop=sp),
              reads=[sq_bufs[c % 2], cb], writes=[s2[1]])

    def mm_s1(e):
        last = None
        for c in range(NC):
            last = e.matmul(s1[0][0:1, :], ctx.ones_col,
                            src_tile[:, c * TT:(c + 1) * TT],
                            start=(c == 0), stop=(c == NC - 1))
        return last
    em.op("tensor", mm_s1, reads=[src_buf, cb], writes=[s1[1]])
    r_mean = row[0:1, 0 * TT:1 * TT]; r_rstd = row[0:1, 1 * TT:2 * TT]
    r_m2 = row[0:1, 2 * TT:3 * TT]; r_b = row[0:1, 2 * TT:3 * TT]
    em.op("vector", lambda e, o=r_mean, i=s1[0][0:1, :]:
          e.tensor_scalar_mul(o, i, 1.0 / D), reads=[s1[1]], writes=[row_buf])
    em.op("vector", lambda e, o=r_m2, i=r_mean: e.tensor_mul(o, i, i),
          reads=[row_buf], writes=[row_buf])
    em.op("vector", lambda e, o=r_rstd, i0=s2[0][0:1, :], i1=r_m2:
          e.scalar_tensor_tensor(o, i0, 1.0 / D, i1, ALU.mult, ALU.subtract),
          reads=[s2[1], row_buf], writes=[row_buf])
    em.op("vector", lambda e, o=r_rstd, i=r_rstd: e.tensor_scalar_add(o, i, eps),
          reads=[row_buf], writes=[row_buf])
    em.op("scalar", lambda e, o=r_rstd, i=r_rstd: e.activation(o, i, AF.Sqrt),
          reads=[row_buf], writes=[row_buf])
    em.op("vector", lambda e, o=r_rstd, i=r_rstd: e.reciprocal(o, i),
          reads=[row_buf], writes=[row_buf])
    em.op("vector", lambda e, o=r_b, i0=r_mean, i1=r_rstd:
          e.scalar_tensor_tensor(o, i0, -1.0, i1, ALU.mult, ALU.mult),
          reads=[row_buf], writes=[row_buf])
    em.op("tensor", lambda e, o=ba[0][:, :], r=r_rstd:
          e.matmul(o, ctx.ones_row, r, start=True, stop=True),
          reads=[row_buf, cb], writes=[ba[1]])
    em.op("tensor", lambda e, o=bb_[0][:, :], r=r_b:
          e.matmul(o, ctx.ones_row, r, start=True, stop=True),
          reads=[row_buf, cb], writes=[bb_[1]])
    for c in range(NC):
        xa = src_tile[:, c * TT:(c + 1) * TT]
        tmp_ = sq[:, (c % 2) * TT:(c % 2) * TT + TT]
        em.op("vector", lambda e, o=tmp_, i0=xa, i1=ba[0][:, :]:
              e.tensor_mul(o, i0, i1), reads=[src_buf, ba[1]], writes=[sq_bufs[c % 2]])
        em.op("vector", lambda e, o=tmp_, i0=tmp_, i1=bb_[0][:, :]:
              e.tensor_add(o, i0, i1), reads=[sq_bufs[c % 2], bb_[1]],
              writes=[sq_bufs[c % 2]])
        em.dma("gpsimd", yout_ap[c, :, t0_out:t0_out + TT], tmp_,
               reads=[sq_bufs[c % 2]])


def build_moe(kind):
    nc = bass.Bass("TRN2", target_bir_lowering=False, debug=False)
    xin = nc.dram_tensor("xin", [NC, 128, T], F32, kind="ExternalInput").ap()
    consts = nc.dram_tensor("consts", [128, CONSTW], F32, kind="ExternalInput").ap()
    yout = nc.dram_tensor("yout", [NC, 128, T], F32, kind="ExternalOutput").ap()
    gw = nc.dram_tensor("gw", [D, 8], F32, kind="ExternalInput").ap()
    acc_dr = nc.dram_tensor("acc_dr", [NC, 128, T], F32).ap()
    if kind == "l2":
        gbr = nc.dram_tensor("gbr", [128, 8], F32, kind="ExternalInput").ap()
        w1 = nc.dram_tensor("w1", [8, D, 4096], F32, kind="ExternalInput").ap()
        b1 = nc.dram_tensor("b1", [128, 8 * 32], F32, kind="ExternalInput").ap()
        w2 = nc.dram_tensor("w2", [8, 4096, D], F32, kind="ExternalInput").ap()
        b2 = nc.dram_tensor("b2", [8, D], F32, kind="ExternalInput").ap()
        FF, NOC = 4096, 32
    else:
        ew1 = nc.dram_tensor("ew1", [8, D, 2048], F32, kind="ExternalInput").ap()
        ew2 = nc.dram_tensor("ew2", [8, D, 2048], F32, kind="ExternalInput").ap()
        ew3 = nc.dram_tensor("ew3", [8, 2048, D], F32, kind="ExternalInput").ap()
        sw1 = nc.dram_tensor("sw1", [D, 2048], F32, kind="ExternalInput").ap()
        sw2 = nc.dram_tensor("sw2", [D, 2048], F32, kind="ExternalInput").ap()
        sw3 = nc.dram_tensor("sw3", [2048, D], F32, kind="ExternalInput").ap()
        FF, NOC = 2048, 16

    with contextlib.ExitStack() as stack:
        ctx = Ctx(nc, stack)
        em = ctx.em
        xb = ctx.sbuf("xb", [128, NC * T], BF16)
        hh = ctx.sbuf("hh", [128, NOC * TT], BF16)
        ctx.sbuf("wsA", [128, 4096], BF16)
        ctx.sbuf("wsB", [128, 4096], BF16)
        ctx.sbuf("wgA", [128, 4096], F32)
        ctx.sbuf("wgB", [128, 4096], F32)
        ctx.sbuf("wbc", [128, 2 * TT], F32, nbuf=2)
        tmp = ctx.sbuf("tmp", [128, 2 * TT], F32, nbuf=2)
        sq = ctx.sbuf("sq", [128, 2 * TT], F32, nbuf=2)
        zt = ctx.sbuf("zt", [128, 2 * TT], F32, nbuf=2)
        row = ctx.sbuf("row", [1, 3 * TT], F32)
        xres = ctx.sbuf("xres", [128, NC * TT], F32)
        accT = ctx.sbuf("accT", [128, NC * TT], F32)
        wrow1 = ctx.sbuf("wrow1", [1, T], F32)
        if kind == "l3":
            s_all = ctx.sbuf("s_all", [128, NOC * TT], BF16)
        ctx.alloc_psum(8)
        ctx.load_consts(consts)
        cb = ctx.cb
        xbb = ctx.bufs["xb"]
        hhb = ctx.bufs["hh"]
        xresb = ctx.bufs["xres"]
        accTb = ctx.bufs["accT"]
        acc_bufs = {(c, tt): em.buf(f"acc_{c}_{tt}") for c in range(NC) for tt in range(NTT)}

        xf_stage = ctx.tiles["wgA"]
        for c in range(NC):
            for h_ in range(2):
                em.dma("sync", xf_stage[:, h_ * 1024:(h_ + 1) * 1024],
                       xin[c][:, h_ * 1024:(h_ + 1) * 1024], writes=[ctx.bufs["wgA"]])
                em.op("vector", lambda e, o=xb[:, c * T + h_ * 1024:c * T + (h_ + 1) * 1024],
                      i=xf_stage[:, h_ * 1024:(h_ + 1) * 1024]:
                      e.tensor_copy(o, i), reads=[ctx.bufs["wgA"]], writes=[xbb])

        if kind == "l2":
            gbr_t = ctx.sbuf("gbr_t", [128, 8], F32)
            em.dma("sync", gbr_t[:, :], gbr, writes=[ctx.bufs["gbr_t"]])
            b1t = ctx.sbuf("b1t", [128, 8 * 32], F32)
            em.dma("sync", b1t[:, :], b1, writes=[ctx.bufs["b1t"]])
            b2f = ctx.sbuf("b2f", [8, D], F32)
            b2t = ctx.sbuf("b2t", [8, D], BF16)
            em.dma("sync", b2f[:, :], b2, writes=[ctx.bufs["b2f"]])
            em.op("vector", lambda e, o=b2t[:, :], i=b2f[:, :]: e.tensor_copy(o, i),
                  reads=[ctx.bufs["b2f"]], writes=[ctx.bufs["b2t"]])
            wrdr, wrdr_buf, wT_sb, wTb = emit_gate(ctx, xb, xbb, gw,
                                                   gb_rep=gbr_t[:, :],
                                                   gb_buf=ctx.bufs["gbr_t"])
            wTbf = ctx.sbuf("wTbf", [8, T], BF16)
            em.op("vector", lambda e, o=wTbf[:, :], i=wT_sb[:, :]: e.tensor_copy(o, i),
                  reads=[wTb], writes=[ctx.bufs["wTbf"]])
        else:
            wrdr, wrdr_buf, wT_sb, wTb = emit_gate(ctx, xb, xbb, gw)

        gst = GemmState()
        n_exp_total = 9 if kind == "l3" else 8
        for e_ in range(n_exp_total):
            shared = (kind == "l3" and e_ == 8)
            if not shared:
                em.dma("sync", wrow1[0:1, :], wrdr[e_:e_ + 1, :],
                       reads=[wrdr_buf], writes=[ctx.bufs["wrow1"]])
            for tt in range(NTT):
                t0 = tt * TT
                if not shared:
                    wbc_ap, wbc_b = emit_wbc(ctx, wrow1, ctx.bufs["wrow1"], 0, t0, tt % 2)
                if kind == "l2":
                    def evac1(nout0, tt_, pt, pb, e_=e_, wbc_ap=wbc_ap, wbc_b=wbc_b):
                        oc = nout0 // 128
                        tpar = oc % 2
                        tdst = tmp[:, tpar * TT:(tpar + 1) * TT]
                        em.op("scalar", lambda e, o=tdst, i=pt[0:128, 0:TT],
                              b=b1t[:, e_ * 32 + oc:e_ * 32 + oc + 1]:
                              e.activation(o, i, AF.Gelu, bias=b),
                              reads=[pb, ctx.bufs["b1t"]], writes=[ctx.bufs["tmp"][tpar]])
                        em.op("vector", lambda e, o=hh[:, oc * TT:(oc + 1) * TT],
                              i0=tdst, i1=wbc_ap: e.tensor_mul(o, i0, i1),
                              reads=[ctx.bufs["tmp"][tpar], wbc_b], writes=[hhb])
                    emit_gemm(ctx, x_tile=xb, x_buf=xbb, x_col0=0, Kdim=D,
                              w_dram_ap=w1[e_], w_buf=em.buf(), Ndim=4096, evac=evac1,
                              slab_n=512, tts=[tt], psum_ids=(0, 1), gst=gst, w_f32=True)
                else:
                    wa1 = sw1 if shared else ew1[e_]
                    wa2 = sw2 if shared else ew2[e_]
                    def evac_h1(nout0, tt_, pt, pb):
                        oc = nout0 // 128
                        sdst = s_all[:, oc * TT:(oc + 1) * TT]
                        em.op("scalar", lambda e, o=sdst, i=pt[0:128, 0:TT]:
                              e.activation(o, i, AF.Silu),
                              reads=[pb], writes=[ctx.bufs["s_all"]])
                    emit_gemm(ctx, x_tile=xb, x_buf=xbb, x_col0=0, Kdim=D,
                              w_dram_ap=wa1, w_buf=em.buf(), Ndim=2048, evac=evac_h1,
                              slab_n=512, tts=[tt], psum_ids=(0, 1), gst=gst, w_f32=True)
                    if shared:
                        def evac_h2(nout0, tt_, pt, pb):
                            oc = nout0 // 128
                            em.op("vector", lambda e, o=hh[:, oc * TT:(oc + 1) * TT],
                                  i0=s_all[:, oc * TT:(oc + 1) * TT], i1=pt[0:128, 0:TT]:
                                  e.tensor_mul(o, i0, i1),
                                  reads=[ctx.bufs["s_all"], pb], writes=[hhb])
                    else:
                        def evac_h2(nout0, tt_, pt, pb, wbc_ap=wbc_ap, wbc_b=wbc_b):
                            oc = nout0 // 128
                            tpar = oc % 2
                            sdst = sq[:, tpar * TT:(tpar + 1) * TT]
                            em.op("vector", lambda e, o=sdst, i0=pt[0:128, 0:TT],
                                  i1=wbc_ap: e.tensor_mul(o, i0, i1),
                                  reads=[pb, wbc_b], writes=[ctx.bufs["sq"][tpar]])
                            em.op("vector", lambda e, o=hh[:, oc * TT:(oc + 1) * TT],
                                  i0=s_all[:, oc * TT:(oc + 1) * TT], i1=sdst:
                                  e.tensor_mul(o, i0, i1),
                                  reads=[ctx.bufs["s_all"], ctx.bufs["sq"][tpar]],
                                  writes=[hhb])
                    emit_gemm(ctx, x_tile=xb, x_buf=xbb, x_col0=0, Kdim=D,
                              w_dram_ap=wa2, w_buf=em.buf(), Ndim=2048, evac=evac_h2,
                              slab_n=512, tts=[tt], psum_ids=(2, 3), gst=gst, w_f32=True)

                def evac_last(nout0, tt_, pt, pb, tt=tt, e_=e_):
                    oc = nout0 // 128
                    zpar = oc % 2
                    zdst = zt[:, zpar * TT:(zpar + 1) * TT]
                    em.op("vector", lambda e, o=zdst, i=pt[0:128, 0:TT]:
                          e.tensor_copy(o, i), reads=[pb], writes=[ctx.bufs["zt"][zpar]])
                    kw = {} if e_ == 0 else {"accum_op": ALU.add}
                    em.dma("gpsimd", acc_dr[oc][:, tt * TT:(tt + 1) * TT], zdst,
                           reads=[ctx.bufs["zt"][zpar]], writes=[acc_bufs[(oc, tt)]], **kw)
                wlast = (w2[e_] if kind == "l2" else (sw3 if shared else ew3[e_]))
                emit_gemm(ctx, x_tile=hh, x_buf=hhb, x_col0=0, Kdim=FF,
                          w_dram_ap=wlast, w_buf=em.buf(), Ndim=D, evac=evac_last,
                          slab_n=4096 // (FF // 128), tts=[0], psum_ids=(4, 5),
                          tile_T=TT, gst=gst, w_f32=True)

        for tt in range(NTT):
            t0 = tt * TT
            for c in range(NC):
                em.dma("sync", accT[:, c * TT:(c + 1) * TT], acc_dr[c][:, t0:t0 + TT],
                       reads=[acc_bufs[(c, tt)]], writes=[accTb])
                em.dma("sync", xres[:, c * TT:(c + 1) * TT], xin[c][:, t0:t0 + TT],
                       writes=[xresb])
            for c in range(NC):
                dst = accT[:, c * TT:(c + 1) * TT]
                em.op("vector", lambda e, o=dst, i0=dst, i1=xres[:, c * TT:(c + 1) * TT]:
                      e.tensor_add(o, i0, i1), reads=[xresb, accTb], writes=[accTb])
                if kind == "l2":
                    pt, pb = ctx.psum[(c % 2)]
                    em.op("tensor", lambda e, o=pt[0:128, 0:TT],
                          l=b2t[0:8, c * 128:(c + 1) * 128],
                          r=wTbf[0:8, t0:t0 + TT]:
                          e.matmul(o, l, r, start=True, stop=True),
                          reads=[ctx.bufs["b2t"], ctx.bufs["wTbf"]], writes=[pb])
                    em.op("vector", lambda e, o=dst, i0=dst, i1=pt[0:128, 0:TT]:
                          e.tensor_add(o, i0, i1), reads=[pb, accTb], writes=[accTb])
            emit_ln_tile(ctx, accT, accTb, yout, t0)
        em.finalize()
    return nc


def build_l2_ffn():
    return build_moe("l2")


def build_l3_ffn():
    return build_moe("l3")


def prep_moe(kind, inputs, x):
    shards = shard_x(x)
    if kind == "l2":
        base = {
            "gw": np.ascontiguousarray(inputs["L2_ffn_Wg"], np.float32),
            "gbr": np.ascontiguousarray(
                np.broadcast_to(np.asarray(inputs["L2_ffn_bg"], np.float32), (128, 8))),
            "w1": np.ascontiguousarray(inputs["L2_ffn_W1"], np.float32),
            "b1": np.ascontiguousarray(
                np.asarray(inputs["L2_ffn_b1"], np.float32).reshape(8 * 32, 128).T
            ).reshape(128, 8 * 32),
            "w2": np.ascontiguousarray(inputs["L2_ffn_W2"], np.float32),
            "b2": np.ascontiguousarray(inputs["L2_ffn_b2"], np.float32),
            "consts": host_consts(),
        }
    else:
        base = {
            "gw": np.ascontiguousarray(inputs["L3_ffn_gate_w"], np.float32),
            "ew1": np.ascontiguousarray(inputs["L3_ffn_ew1"], np.float32),
            "ew2": np.ascontiguousarray(inputs["L3_ffn_ew2"], np.float32),
            "ew3": np.ascontiguousarray(inputs["L3_ffn_ew3"], np.float32),
            "sw1": np.ascontiguousarray(inputs["L3_ffn_sw1"], np.float32),
            "sw2": np.ascontiguousarray(inputs["L3_ffn_sw2"], np.float32),
            "sw3": np.ascontiguousarray(inputs["L3_ffn_sw3"], np.float32),
            "consts": host_consts(),
        }
    return [dict(base, xin=shards[i]) for i in range(8)]


def stage_l2_ffn(inputs, x):
    res = run_stage(build_l2_ffn, prep_moe("l2", inputs, x))
    return unshard_x([np.asarray(r["yout"]) for r in res])


def stage_l3_ffn(inputs, x):
    res = run_stage(build_l3_ffn, prep_moe("l3", inputs, x))
    return unshard_x([np.asarray(r["yout"]) for r in res])





def build_l1_attn():
    nc = bass.Bass("TRN2", target_bir_lowering=False, debug=False)
    xin = nc.dram_tensor("xin", [NC, 128, T], F32, kind="ExternalInput").ap()
    consts = nc.dram_tensor("consts", [128, CONSTW], F32, kind="ExternalInput").ap()
    yout = nc.dram_tensor("yout", [NC, 128, T], F32, kind="ExternalOutput").ap()
    qw = nc.dram_tensor("qw", [D, D], F32, kind="ExternalInput").ap()
    kvw = nc.dram_tensor("kvw", [D, 128], F32, kind="ExternalInput").ap()
    wo = nc.dram_tensor("wo", [D, D], F32, kind="ExternalInput").ap()
    bo = nc.dram_tensor("bo", [128, 8], F32, kind="ExternalInput").ap()

    with contextlib.ExitStack() as stack:
        ctx = Ctx(nc, stack)
        em = ctx.em
        xb = ctx.sbuf("xb", [128, NC * T], BF16)
        ctx.sbuf("wsA", [128, 4096], BF16)
        ctx.sbuf("wsB", [128, 4096], BF16)
        ctx.sbuf("wgA", [128, 4096], F32)
        ctx.sbuf("wgB", [128, 4096], F32)
        kvf = ctx.sbuf("kvf", [128, NC * 128], F32)
        kvb = ctx.sbuf("kvb", [128, NC * 128], BF16)
        qbf = ctx.sbuf("qbf", [128, NC * TT], BF16)
        kbf = ctx.sbuf("kbf", [128, TT], BF16)
        vtm = ctx.sbuf("vtm", [128, 4 * 64], BF16)
        pt_sb = ctx.sbuf("pt_sb", [128, 2 * 128], BF16)
        psb = ctx.sbuf("psb", [128, 256], BF16)
        mrow = ctx.sbuf("mrow", [128, 4], F32)
        afm = ctx.sbuf("afm", [128, NC * TT], BF16)
        tmp = ctx.sbuf("tmp", [128, 2 * TT], F32, nbuf=2)
        sq = ctx.sbuf("sq", [128, 2 * TT], F32, nbuf=2)
        row = ctx.sbuf("row", [1, 3 * TT], F32)
        xres = ctx.sbuf("xres", [128, NC * TT], F32)
        accT = ctx.sbuf("accT", [128, NC * TT], F32)
        bot = ctx.sbuf("bot", [128, 8], F32)
        ctx.alloc_psum(7, 1)
        ctx.load_consts(consts)
        cb = ctx.cb
        xbb = ctx.bufs["xb"]
        qb = ctx.bufs["qbf"]; kb_ = ctx.bufs["kbf"]; vb = ctx.bufs["vtm"]
        pb_sb = ctx.bufs["pt_sb"]; psbb = ctx.bufs["psb"]; mb = ctx.bufs["mrow"]
        ab = ctx.bufs["afm"]; accTb = ctx.bufs["accT"]; xresb = ctx.bufs["xres"]

        em.dma("sync", bot[:, :], bo, writes=[ctx.bufs["bot"]])
        xst = ctx.tiles["wgA"]
        for c in range(NC):
            for h_ in range(2):
                em.dma("sync", xst[:, h_ * 1024:(h_ + 1) * 1024],
                       xin[c][:, h_ * 1024:(h_ + 1) * 1024], writes=[ctx.bufs["wgA"]])
                em.op("vector", lambda e, o=xb[:, c * T + h_ * 1024:c * T + (h_ + 1) * 1024],
                      i=xst[:, h_ * 1024:(h_ + 1) * 1024]:
                      e.tensor_copy(o, i), reads=[ctx.bufs["wgA"]], writes=[xbb])
        em.dma("sync", kvf[:, :].rearrange("p (c n) -> p c n", n=128),
               kvw.rearrange("(c p) n -> p c n", p=128), writes=[ctx.bufs["kvf"]])
        em.op("vector", lambda e, o=kvb[:, :], i=kvf[:, :]: e.tensor_copy(o, i),
              reads=[ctx.bufs["kvf"]], writes=[ctx.bufs["kvb"]])

        gst = GemmState()
        for tt in range(NTT):
            t0 = tt * TT
            def evq(nout0, tt_, pt, pb):
                oc = nout0 // 128
                em.op("scalar", lambda e, o=qbf[:, oc * TT:(oc + 1) * TT],
                      i=pt[0:128, 0:TT]: e.activation(o, i, AF.Identity, scale=0.125),
                      reads=[pb], writes=[qb])
            emit_gemm(ctx, x_tile=xb, x_buf=xbb, x_col0=0, Kdim=D, w_dram_ap=qw,
                      w_buf=em.buf(), Ndim=D, evac=evq, slab_n=512, tts=[tt],
                      psum_ids=(0, 1), gst=gst, w_f32=True)
            def mmk(e, t0=t0):
                last = None
                for kc in range(NC):
                    last = e.matmul(ctx.psum[2][0][0:64, 0:TT],
                                    kvb[:, kc * 128:kc * 128 + 64],
                                    xb[:, kc * T + t0:kc * T + t0 + TT],
                                    start=(kc == 0), stop=(kc == NC - 1))
                return last
            em.op("tensor", mmk, reads=[ctx.bufs["kvb"], xbb], writes=[ctx.psum[2][1]])
            em.op("scalar", lambda e, o=kbf[0:64, :], i=ctx.psum[2][0][0:64, 0:TT]:
                  e.copy(o, i), reads=[ctx.psum[2][1]], writes=[kb_])
            em.op("scalar", lambda e, o=kbf[64:128, :], i=kbf[0:64, :]:
                  e.copy(o, i), reads=[kb_], writes=[kb_])
            for st_ in range(4):
                def mmv(e, st_=st_, t0=t0):
                    last = None
                    for kc in range(NC):
                        last = e.matmul(ctx.psum[2][0][0:128, 0:64],
                                        xb[:, kc * T + t0 + st_ * 128:
                                           kc * T + t0 + st_ * 128 + 128],
                                        kvb[:, kc * 128 + 64:kc * 128 + 128],
                                        start=(kc == 0), stop=(kc == NC - 1))
                    return last
                em.op("tensor", mmv, reads=[ctx.bufs["kvb"], xbb], writes=[ctx.psum[2][1]])
                em.op("scalar", lambda e, o=vtm[:, st_ * 64:(st_ + 1) * 64],
                      i=ctx.psum[2][0][0:128, 0:64]: e.copy(o, i),
                      reads=[ctx.psum[2][1]], writes=[vb])
            for h in range(16):
                hc, hr = h // 2, (h % 2) * 64
                for qt in range(4):
                    kb2 = qt // 2
                    sp, spb = ctx.psum[2]
                    em.op("tensor", lambda e, o=sp[0:128, 0:256],
                          l=qbf[hr:hr + 64, hc * TT + qt * 128:hc * TT + qt * 128 + 128],
                          r=kbf[hr:hr + 64, kb2 * 256:kb2 * 256 + 256]:
                          e.matmul(o, l, r, start=True, stop=True),
                          reads=[qb, kb_], writes=[spb])
                    em.op("vector", lambda e, o=mrow[:, 0:1], i=sp[0:128, 0:256]:
                          e.reduce_max(o, i, axis=AX.X, negate=True),
                          reads=[spb], writes=[mb])
                    em.op("scalar", lambda e, o=psb[:, :], i=sp[0:128, 0:256],
                          b=mrow[:, 0:1]: e.activation(o, i, AF.Exp, bias=b),
                          reads=[spb, mb], writes=[psbb])
                    em.op("vector", lambda e, o=mrow[:, 1:2], i=psb[:, :]:
                          e.reduce_sum(o, i, axis=AX.X), reads=[psbb], writes=[mb])
                    em.op("vector", lambda e, o=mrow[:, 1:2], i=mrow[:, 1:2]:
                          e.reciprocal(o, i), reads=[mb], writes=[mb])
                    em.op("vector", lambda e, o=psb[:, :], i=psb[:, :], m=mrow[:, 1:2]:
                          e.tensor_single_scalar(o, i, m, ALU.mult),
                          reads=[psbb, mb], writes=[psbb])
                    tb_, tbb = ctx.psum_bf[0]
                    for kc2 in range(2):
                        em.op("tensor", lambda e, o=tb_[0:128, kc2 * 128:(kc2 + 1) * 128],
                              i=psb[:, kc2 * 128:(kc2 + 1) * 128], idt=ctx.identb:
                              e.transpose(o, i, idt), reads=[psbb, cb], writes=[tbb])
                    em.op("vector", lambda e, o=pt_sb[:, :], i=tb_[0:128, 0:256]:
                          e.tensor_copy(o, i), reads=[tbb], writes=[pb_sb])
                    op, opb = ctx.psum[3]
                    def mmo(e, kb2=kb2, hr=hr, hc=hc, qt=qt):
                        last = None
                        for kc2 in range(2):
                            last = e.matmul(op[0:64, 0:128],
                                            vtm[:, (kb2 * 2 + kc2) * 64:(kb2 * 2 + kc2 + 1) * 64],
                                            pt_sb[:, kc2 * 128:(kc2 + 1) * 128],
                                            start=(kc2 == 0), stop=(kc2 == 1))
                        return last
                    em.op("tensor", mmo, reads=[pb_sb, vb], writes=[opb])
                    em.op("scalar", lambda e,
                          o=afm[hr:hr + 64, hc * TT + qt * 128:hc * TT + qt * 128 + 128],
                          i=op[0:64, 0:128]: e.copy(o, i), reads=[opb], writes=[ab])
            for c in range(NC):
                em.dma("sync", xres[:, c * TT:(c + 1) * TT], xin[c][:, t0:t0 + TT],
                       writes=[xresb])
            def evo(nout0, tt_, pt, pb):
                oc = nout0 // 128
                dst = accT[:, oc * TT:(oc + 1) * TT]
                em.op("vector", lambda e, o=dst, i=pt[0:128, 0:TT],
                      b=bot[:, oc:oc + 1]:
                      e.tensor_scalar(o, i, b, None, ALU.add),
                      reads=[pb, ctx.bufs["bot"]], writes=[accTb])
                em.op("vector", lambda e, o=dst, i0=dst, i1=xres[:, oc * TT:(oc + 1) * TT]:
                      e.tensor_add(o, i0, i1), reads=[accTb, xresb], writes=[accTb])
            emit_gemm(ctx, x_tile=afm, x_buf=ab, x_col0=0, Kdim=D, w_dram_ap=wo,
                      w_buf=em.buf(), Ndim=D, evac=evo, slab_n=512, tts=[0],
                      psum_ids=(0, 1), tile_T=TT, gst=gst, w_f32=True)
            emit_ln_tile(ctx, accT, accTb, yout, t0)
        em.finalize()
    return nc


def prep_l1_attn(inputs, x):
    shards = shard_x(x)
    base = {
        "qw": np.ascontiguousarray(np.asarray(inputs["L1_attn_q_w"], np.float32).T),
        "kvw": np.ascontiguousarray(np.asarray(inputs["L1_attn_kv_w"], np.float32).T),
        "wo": np.ascontiguousarray(np.asarray(inputs["L1_attn_proj_w"], np.float32).T),
        "bo": chunk_vec(np.asarray(inputs["L1_attn_proj_b"], np.float32)),
        "consts": host_consts(),
    }
    return [dict(base, xin=shards[i]) for i in range(8)]


def stage_l1_attn(inputs, x):
    res = run_stage(build_l1_attn, prep_l1_attn(inputs, x))
    return unshard_x([np.asarray(r["yout"]) for r in res])





TEXT = T + 512


def build_l3_attn():
    nc = bass.Bass("TRN2", target_bir_lowering=False, debug=False)
    xin = nc.dram_tensor("xin", [NC, 128, TEXT], F32, kind="ExternalInput").ap()
    consts = nc.dram_tensor("consts", [128, CONSTW], F32, kind="ExternalInput").ap()
    yout = nc.dram_tensor("yout", [NC, 128, T], F32, kind="ExternalOutput").ap()
    wq = nc.dram_tensor("wq", [D, D], F32, kind="ExternalInput").ap()
    wk = nc.dram_tensor("wk", [D, D], F32, kind="ExternalInput").ap()
    wv = nc.dram_tensor("wv", [D, D], F32, kind="ExternalInput").ap()
    wo = nc.dram_tensor("wo", [D, D], F32, kind="ExternalInput").ap()
    bq = nc.dram_tensor("bq", [128, 8], F32, kind="ExternalInput").ap()
    bk = nc.dram_tensor("bk", [128, 8], F32, kind="ExternalInput").ap()
    bvr = nc.dram_tensor("bvr", [128, D], F32, kind="ExternalInput").ap()
    bo = nc.dram_tensor("bo", [128, 8], F32, kind="ExternalInput").ap()
    bias = nc.dram_tensor("bias", [16, 128, 640], F32, kind="ExternalInput").ap()

    with contextlib.ExitStack() as stack:
        ctx = Ctx(nc, stack)
        em = ctx.em
        xb = ctx.sbuf("xb", [128, NC * TEXT], BF16)
        kf = ctx.sbuf("kf", [128, NC * TEXT], BF16)
        vt = ctx.sbuf("vt", [128, (TEXT // 128) * D], BF16)
        qbf = ctx.sbuf("qbf", [128, NC * TT], BF16)
        ctx.sbuf("wsA", [128, 2048], BF16)
        ctx.sbuf("wsB", [128, 2048], BF16)
        ctx.sbuf("wgA", [128, 2048], F32)
        ctx.sbuf("wgB", [128, 2048], F32)
        bvt = ctx.sbuf("bvt", [128, D], F32)
        s_sb = ctx.sbuf("s_sb", [128, 640], F32)
        psb = ctx.sbuf("psb", [128, 640], BF16)
        pt_sb = ctx.sbuf("pt_sb", [128, 5 * 128], BF16)
        mrow = ctx.sbuf("mrow", [128, 4], F32)
        afm = ctx.sbuf("afm", [128, NC * TT], BF16)
        bias_t = ctx.sbuf("bias_t", [128, 640], F32)
        tmp = ctx.sbuf("tmp", [128, 2 * TT], F32, nbuf=2)
        sq = ctx.sbuf("sq", [128, 2 * TT], F32, nbuf=2)
        row = ctx.sbuf("row", [1, 3 * TT], F32)
        accT = ctx.sbuf("accT", [128, NC * TT], F32)
        bqt = ctx.sbuf("bqt", [128, 8], F32)
        bkt = ctx.sbuf("bkt", [128, 8], F32)
        bot = ctx.sbuf("bot", [128, 8], F32)
        ctx.alloc_psum(7, 1)
        ctx.load_consts(consts)
        cb = ctx.cb
        xbb = ctx.bufs["xb"]
        kfb = ctx.bufs["kf"]; vtb = ctx.bufs["vt"]; qb = ctx.bufs["qbf"]
        psbb = ctx.bufs["psb"]; pb_sb = ctx.bufs["pt_sb"]; mb = ctx.bufs["mrow"]
        ab = ctx.bufs["afm"]; accTb = ctx.bufs["accT"]
        ssb = ctx.bufs["s_sb"]

        for nm, ap_ in (("bqt", bq), ("bkt", bk), ("bot", bo), ("bvt", bvr)):
            em.dma("sync", ctx.tiles[nm][:, :], ap_, writes=[ctx.bufs[nm]])
        em.op("vector", lambda e, o=bqt[:, :], i=bqt[:, :]: e.tensor_scalar_mul(o, i, 0.125),
              reads=[ctx.bufs["bqt"]], writes=[ctx.bufs["bqt"]])
        xst = ctx.tiles["wgA"]
        for c in range(NC):
            for h_ in range(3):
                c0, c1 = h_ * 1024, min(TEXT, (h_ + 1) * 1024)
                em.dma("sync", xst[:, 0:c1 - c0], xin[c][:, c0:c1], writes=[ctx.bufs["wgA"]])
                em.op("vector", lambda e, o=xb[:, c * TEXT + c0:c * TEXT + c1],
                      i=xst[:, 0:c1 - c0]:
                      e.tensor_copy(o, i), reads=[ctx.bufs["wgA"]], writes=[xbb])
        gst = GemmState()
        def evk(nout0, tt_, pt, pb):
            oc = nout0 // 128
            em.op("scalar", lambda e, o=kf[:, oc * TEXT + tt_ * TT:oc * TEXT + tt_ * TT + TT],
                  i=pt[0:128, 0:TT], b=bkt[:, oc:oc + 1]:
                  e.activation(o, i, AF.Identity, bias=b),
                  reads=[pb, ctx.bufs["bkt"]], writes=[kfb])
        emit_gemm(ctx, x_tile=xb, x_buf=xbb, x_col0=0, Kdim=D, w_dram_ap=wk,
                  w_buf=em.buf(), Ndim=D, evac=evk, slab_n=256, tts=range(TEXT // TT),
                  psum_ids=(0, 1), tile_T=TEXT, gst=gst, w_f32=True)
        for vs in range(4):
            n0 = vs * 256
            w, wb = ((ctx.tiles["wsA"], ctx.bufs["wsA"]) if vs % 2 == 0
                     else (ctx.tiles["wsB"], ctx.bufs["wsB"]))
            g, gb = ((ctx.tiles["wgA"], ctx.bufs["wgA"]) if vs % 2 == 0
                     else (ctx.tiles["wgB"], ctx.bufs["wgB"]))
            em.dma("sync", g[:, 0:NC * 256].rearrange("p (c n) -> p c n", n=256),
                   wv[:, n0:n0 + 256].rearrange("(c p) n -> p c n", p=128),
                   reads=[], writes=[gb])
            em.op("scalar", lambda e, o=w[:, 0:NC * 256], i=g[:, 0:NC * 256]:
                  e.copy(o, i), reads=[gb], writes=[wb])
            for st_ in range(TEXT // 128):
                pt, pb = ctx.psum[2]
                def mmv(e, st_=st_, w=w, n0=n0, pt=pt):
                    last = None
                    for kc in range(NC):
                        last = e.matmul(pt[0:128, 0:256],
                                        xb[:, kc * TEXT + st_ * 128:kc * TEXT + st_ * 128 + 128],
                                        w[:, kc * 256:(kc + 1) * 256],
                                        start=(kc == 0), stop=(kc == NC - 1))
                    return last
                em.op("tensor", mmv, reads=[wb, xbb], writes=[pb])
                dst = vt[:, st_ * D + n0: st_ * D + n0 + 256]
                tpar = st_ % 2
                em.op("vector", lambda e, o=tmp[:, tpar * TT:tpar * TT + 256],
                      i=pt[0:128, 0:256], i1=bvt[:, n0:n0 + 256]:
                      e.tensor_add(o, i, i1), reads=[pb, ctx.bufs["bvt"]],
                      writes=[ctx.bufs["tmp"][tpar]])
                em.op("vector", lambda e, o=dst, i=tmp[:, tpar * TT:tpar * TT + 256]:
                      e.tensor_copy(o, i), reads=[ctx.bufs["tmp"][tpar]], writes=[vtb])

        for tt in range(NTT):
            t0 = tt * TT
            def evq(nout0, tt_, pt, pb):
                oc = nout0 // 128
                em.op("scalar", lambda e, o=qbf[:, oc * TT:(oc + 1) * TT],
                      i=pt[0:128, 0:TT], b=bqt[:, oc:oc + 1]:
                      e.activation(o, i, AF.Identity, bias=b, scale=0.125),
                      reads=[pb, ctx.bufs["bqt"]], writes=[qb])
            emit_gemm(ctx, x_tile=xb, x_buf=xbb, x_col0=256 + t0, Kdim=D, w_dram_ap=wq,
                      w_buf=em.buf(), Ndim=D, evac=evq, slab_n=256, tts=[0],
                      psum_ids=(0, 1), tile_T=TEXT, gst=gst, w_f32=True)
            for qt in range(4):
                qtg = tt * 4 + qt
                bt_ = bias_t[:, 0:640]
                em.dma("sync", bt_, bias[qtg], writes=[ctx.bufs["bias_t"]])
                for h in range(16):
                    hc, hr = h // 2, (h % 2) * 64
                    sp, spb = ctx.psum[2]
                    sp2, spb2 = ctx.psum[3]
                    ext0 = qtg * 128
                    em.op("tensor", lambda e, o=sp[0:128, 0:512],
                          l=qbf[hr:hr + 64, hc * TT + qt * 128:hc * TT + qt * 128 + 128],
                          r=kf[hr:hr + 64, hc * TEXT + ext0:hc * TEXT + ext0 + 512]:
                          e.matmul(o, l, r, start=True, stop=True),
                          reads=[qb, kfb], writes=[spb])
                    em.op("tensor", lambda e, o=sp2[0:128, 0:128],
                          l=qbf[hr:hr + 64, hc * TT + qt * 128:hc * TT + qt * 128 + 128],
                          r=kf[hr:hr + 64, hc * TEXT + ext0 + 512:hc * TEXT + ext0 + 640]:
                          e.matmul(o, l, r, start=True, stop=True),
                          reads=[qb, kfb], writes=[spb2])
                    em.op("vector", lambda e, o=s_sb[:, 0:512], i0=sp[0:128, 0:512],
                          i1=bt_[:, 0:512]: e.tensor_add(o, i0, i1),
                          reads=[spb, ctx.bufs["bias_t"]], writes=[ssb])
                    em.op("vector", lambda e, o=s_sb[:, 512:640], i0=sp2[0:128, 0:128],
                          i1=bt_[:, 512:640]: e.tensor_add(o, i0, i1),
                          reads=[spb2, ctx.bufs["bias_t"]], writes=[ssb])
                    em.op("vector", lambda e, o=mrow[:, 0:1], i=s_sb[:, :]:
                          e.reduce_max(o, i, axis=AX.X, negate=True),
                          reads=[ssb], writes=[mb])
                    em.op("scalar", lambda e, o=psb[:, :], i=s_sb[:, :], b=mrow[:, 0:1]:
                          e.activation(o, i, AF.Exp, bias=b), reads=[ssb, mb], writes=[psbb])
                    em.op("vector", lambda e, o=mrow[:, 1:2], i=psb[:, :]:
                          e.reduce_sum(o, i, axis=AX.X), reads=[psbb], writes=[mb])
                    em.op("vector", lambda e, o=mrow[:, 1:2], i=mrow[:, 1:2]:
                          e.reciprocal(o, i), reads=[mb], writes=[mb])
                    em.op("vector", lambda e, o=psb[:, :], i=psb[:, :], m=mrow[:, 1:2]:
                          e.tensor_single_scalar(o, i, m, ALU.mult),
                          reads=[psbb, mb], writes=[psbb])
                    tb_, tbb = ctx.psum_bf[0]
                    for kc2 in range(5):
                        em.op("tensor", lambda e, o=tb_[0:128, kc2 * 128:(kc2 + 1) * 128],
                              i=psb[:, kc2 * 128:(kc2 + 1) * 128], idt=ctx.identb:
                              e.transpose(o, i, idt), reads=[psbb, cb], writes=[tbb])
                    em.op("vector", lambda e, o=pt_sb[:, 0:640], i=tb_[0:128, 0:640]:
                          e.tensor_copy(o, i), reads=[tbb], writes=[pb_sb])
                    op_, opb = ctx.psum[3]
                    def mmo(e, qtg=qtg, h=h):
                        last = None
                        for kc2 in range(5):
                            last = e.matmul(op_[0:64, 0:128],
                                            vt[:, (qtg + kc2) * D + h * 64:(qtg + kc2) * D + h * 64 + 64],
                                            pt_sb[:, kc2 * 128:(kc2 + 1) * 128],
                                            start=(kc2 == 0), stop=(kc2 == 4))
                        return last
                    em.op("tensor", mmo, reads=[pb_sb, vtb], writes=[opb])
                    em.op("scalar", lambda e,
                          o=afm[hr:hr + 64, hc * TT + qt * 128:hc * TT + qt * 128 + 128],
                          i=op_[0:64, 0:128]: e.copy(o, i), reads=[opb], writes=[ab])
            for c in range(NC):
                em.dma("sync", accT[:, c * TT:(c + 1) * TT],
                       xin[c][:, 256 + t0:256 + t0 + TT], writes=[accTb])
            def evo(nout0, tt_, pt, pb):
                oc = nout0 // 128
                dst = accT[:, oc * TT:(oc + 1) * TT]
                tpar = oc % 2
                tsl = tmp[:, tpar * TT:(tpar + 1) * TT]
                em.op("vector", lambda e, o=tsl, i=pt[0:128, 0:TT], b=bot[:, oc:oc + 1]:
                      e.tensor_scalar(o, i, b, None, ALU.add),
                      reads=[pb, ctx.bufs["bot"]], writes=[ctx.bufs["tmp"][tpar]])
                em.op("vector", lambda e, o=dst, i0=dst, i1=tsl:
                      e.tensor_add(o, i0, i1), reads=[ctx.bufs["tmp"][tpar], accTb],
                      writes=[accTb])
            emit_gemm(ctx, x_tile=afm, x_buf=ab, x_col0=0, Kdim=D, w_dram_ap=wo,
                      w_buf=em.buf(), Ndim=D, evac=evo, slab_n=256, tts=[0],
                      psum_ids=(0, 1), tile_T=TT, gst=gst, w_f32=True)
            emit_ln_tile(ctx, accT, accTb, yout, t0)
        em.finalize()
    return nc


def prep_l3_attn(inputs, x):
    xp = np.zeros((2, 8192 + 512, D), np.float32)
    xp[:, 256:256 + 8192, :] = x
    shards = []
    for core in range(8):
        b, g = core // 4, core % 4
        seg = xp[b, g * T: g * T + TEXT, :]
        shards.append(fm(seg))
    biases = []
    for core in range(8):
        g = core % 4
        base = g * T
        bias = np.zeros((16, 128, 640), np.float32)
        for qtg in range(16):
            i_g = base + qtg * 128 + np.arange(128)[:, None]
            j_g = base + qtg * 128 - 256 + np.arange(640)[None, :]
            bad = (np.abs(i_g - j_g) > 256) | (j_g < 0) | (j_g >= 8192)
            bias[qtg][bad] = -3e38
        biases.append(bias)
    base_in = {
        "wq": np.ascontiguousarray(np.asarray(inputs["L3_attn_wq"], np.float32).T),
        "wk": np.ascontiguousarray(np.asarray(inputs["L3_attn_wk"], np.float32).T),
        "wv": np.ascontiguousarray(np.asarray(inputs["L3_attn_wv"], np.float32).T),
        "wo": np.ascontiguousarray(np.asarray(inputs["L3_attn_wo"], np.float32).T),
        "bq": chunk_vec(np.asarray(inputs["L3_attn_bq"], np.float32)),
        "bk": chunk_vec(np.asarray(inputs["L3_attn_bk"], np.float32)),
        "bvr": np.ascontiguousarray(np.broadcast_to(
            np.asarray(inputs["L3_attn_bv"], np.float32), (128, D))),
        "bo": chunk_vec(np.asarray(inputs["L3_attn_bo"], np.float32)),
        "consts": host_consts(),
    }
    return [dict(base_in, xin=shards[i], bias=biases[i]) for i in range(8)]


def stage_l3_attn(inputs, x):
    res = run_stage(build_l3_attn, prep_l3_attn(inputs, x))
    return unshard_x([np.asarray(r["yout"]) for r in res])





SFULL = 8192


def build_l2_attn():
    nc = bass.Bass("TRN2", target_bir_lowering=False, debug=False)
    xfull = nc.dram_tensor("xfull", [NC, 128, SFULL], F32, kind="ExternalInput").ap()
    xown = nc.dram_tensor("xown", [NC, 128, T], F32, kind="ExternalInput").ap()
    consts = nc.dram_tensor("consts", [128, CONSTW], F32, kind="ExternalInput").ap()
    yout = nc.dram_tensor("yout", [NC, 128, T], F32, kind="ExternalOutput").ap()
    wq = nc.dram_tensor("wq", [D, D], F32, kind="ExternalInput").ap()
    wk = nc.dram_tensor("wk", [D, D], F32, kind="ExternalInput").ap()
    wv = nc.dram_tensor("wv", [D, D], F32, kind="ExternalInput").ap()
    wo = nc.dram_tensor("wo", [D, D], F32, kind="ExternalInput").ap()
    bq = nc.dram_tensor("bq", [128, 8], F32, kind="ExternalInput").ap()
    bk = nc.dram_tensor("bk", [128, 8], F32, kind="ExternalInput").ap()
    bvr = nc.dram_tensor("bvr", [128, D], F32, kind="ExternalInput").ap()
    bo = nc.dram_tensor("bo", [128, 8], F32, kind="ExternalInput").ap()
    kdr = nc.dram_tensor("kdr", [NC, 128, SFULL], BF16).ap()
    vdr = nc.dram_tensor("vdr", [SFULL // 128, 128, D], BF16).ap()
    mdr = nc.dram_tensor("mdr", [64, 128], F32).ap()

    with contextlib.ExitStack() as stack:
        ctx = Ctx(nc, stack)
        em = ctx.em
        xbt = ctx.sbuf("xbt", [128, NC * TT], BF16)
        xst = ctx.sbuf("xst", [128, NC * TT], F32)
        qbf = ctx.sbuf("qbf", [128, NC * T], BF16)
        ctx.sbuf("wsA", [128, 2048], BF16)
        ctx.sbuf("wsB", [128, 2048], BF16)
        ctx.sbuf("wgA", [128, 2048], F32)
        ctx.sbuf("wgB", [128, 2048], F32)
        kst = ctx.sbuf("kst", [128, 2 * TT], BF16, nbuf=2)
        vst = ctx.sbuf("vst", [128, 2 * TT], BF16, nbuf=2)
        ktile = ctx.sbuf("ktile", [65, 2 * SFULL], BF16, nbuf=2)
        vaug = ctx.sbuf("vaug", [128, 2 * 64 * 65], BF16, nbuf=2)
        qaug = ctx.sbuf("qaug", [65, 16 * TT], BF16)
        ptsb = ctx.sbuf("ptsb", [128, 8 * TT], BF16, nbuf=8)
        osb = ctx.sbuf("osb", [65, TT], F32)
        m_all = ctx.sbuf("m_all", [128, 64], F32)
        mT = ctx.sbuf("mT", [64, 128], F32)
        rden = ctx.sbuf("rden", [1, TT], F32)
        onorm = ctx.sbuf("onorm", [128, 64], F32)
        afm = ctx.sbuf("afm", [128, NC * TT], BF16)
        bvt = ctx.sbuf("bvt", [128, D], F32)
        tmp = ctx.sbuf("tmp", [128, 2 * TT], F32, nbuf=2)
        sq = ctx.sbuf("sq", [128, 2 * TT], F32, nbuf=2)
        row = ctx.sbuf("row", [1, 3 * TT], F32)
        accT = ctx.sbuf("accT", [128, NC * TT], F32)
        bqt = ctx.sbuf("bqt", [128, 8], F32)
        bkt = ctx.sbuf("bkt", [128, 8], F32)
        bot = ctx.sbuf("bot", [128, 8], F32)
        ctx.alloc_psum(8)
        ctx.load_consts(consts)
        cb = ctx.cb
        qb = ctx.bufs["qbf"]; ab = ctx.bufs["afm"]; accTb = ctx.bufs["accT"]
        kdrb = em.buf("kdr"); vdrb = em.buf("vdr"); mdrb = em.buf("mdr")
        qaugb = ctx.bufs["qaug"]; mab = ctx.bufs["m_all"]; mTb = ctx.bufs["mT"]

        for nm, ap_ in (("bqt", bq), ("bkt", bk), ("bot", bo), ("bvt", bvr)):
            em.dma("sync", ctx.tiles[nm][:, :], ap_, writes=[ctx.bufs[nm]])
        em.op("vector", lambda e, o=bqt[:, :], i=bqt[:, :]: e.tensor_scalar_mul(o, i, 0.125),
              reads=[ctx.bufs["bqt"]], writes=[ctx.bufs["bqt"]])
        for par in range(2):
            em.op("vector", lambda e, o=ktile[64:65, par * SFULL:(par + 1) * SFULL]:
                  e.memset(o, -1.0), writes=[ctx.bufs["ktile"][par]])
            ones_view = vaug[:, par * 4160:(par + 1) * 4160].rearrange(
                "p (s e) -> p s e", e=65)[:, :, 64:65]
            em.op("vector", lambda e, o=ones_view: e.memset(o, 1.0),
                  writes=[ctx.bufs["vaug"][par]])

        gst = GemmState()
        for ft in range(SFULL // TT):
            f0 = ft * TT
            for c in range(NC):
                em.dma("sync", xst[:, c * TT:(c + 1) * TT], xfull[c][:, f0:f0 + TT],
                       writes=[ctx.bufs["xst"]])
            em.op("vector", lambda e, o=xbt[:, :], i=xst[:, :]: e.tensor_copy(o, i),
                  reads=[ctx.bufs["xst"]], writes=[ctx.bufs["xbt"]])
            def evk(nout0, tt_, pt, pb, ft=ft, par=par):
                oc = nout0 // 128
                kpar = oc % 2
                em.op("scalar", lambda e, o=kst[:, kpar * TT:(kpar + 1) * TT],
                      i=pt[0:128, 0:TT], b=bkt[:, oc:oc + 1]:
                      e.activation(o, i, AF.Identity, bias=b),
                      reads=[pb, ctx.bufs["bkt"]], writes=[ctx.bufs["kst"][kpar]])
                em.dma("gpsimd", kdr[oc][:, ft * TT:(ft + 1) * TT],
                       kst[:, kpar * TT:(kpar + 1) * TT],
                       reads=[ctx.bufs["kst"][kpar]], writes=[kdrb])
            emit_gemm(ctx, x_tile=xbt, x_buf=ctx.bufs["xbt"], x_col0=0, Kdim=D,
                      w_dram_ap=wk, w_buf=em.buf(), Ndim=D, evac=evk, slab_n=256,
                      tts=[0], psum_ids=(0, 1), tile_T=TT, gst=gst, w_f32=True)
            for vs in range(4):
                n0 = vs * 256
                w, wb = ((ctx.tiles["wsA"], ctx.bufs["wsA"]) if vs % 2 == 0
                         else (ctx.tiles["wsB"], ctx.bufs["wsB"]))
                g, gb = ((ctx.tiles["wgA"], ctx.bufs["wgA"]) if vs % 2 == 0
                         else (ctx.tiles["wgB"], ctx.bufs["wgB"]))
                em.dma("sync", g[:, 0:NC * 256].rearrange("p (c n) -> p c n", n=256),
                       wv[:, n0:n0 + 256].rearrange("(c p) n -> p c n", p=128),
                       writes=[gb])
                em.op("scalar", lambda e, o=w[:, 0:NC * 256], i=g[:, 0:NC * 256]:
                      e.copy(o, i), reads=[gb], writes=[wb])
                for st_ in range(4):
                    pt, pb = ctx.psum[2 + st_ % 2]
                    def mmv(e, st_=st_, w=w, pt=pt):
                        last = None
                        for kc in range(NC):
                            last = e.matmul(pt[0:128, 0:256],
                                            xbt[:, kc * TT + st_ * 128:kc * TT + st_ * 128 + 128],
                                            w[:, kc * 256:(kc + 1) * 256],
                                            start=(kc == 0), stop=(kc == NC - 1))
                        return last
                    em.op("tensor", mmv, reads=[wb, ctx.bufs["xbt"]], writes=[pb])
                    vpar = st_ % 2
                    em.op("vector", lambda e, o=vst[:, vpar * TT:vpar * TT + 256],
                          i0=pt[0:128, 0:256], i1=bvt[:, n0:n0 + 256]:
                          e.tensor_add(o, i0, i1), reads=[pb, ctx.bufs["bvt"]],
                          writes=[ctx.bufs["vst"][vpar]])
                    em.dma("gpsimd", vdr[ft * 4 + st_][:, n0:n0 + 256],
                           vst[:, vpar * TT:vpar * TT + 256],
                           reads=[ctx.bufs["vst"][vpar]], writes=[vdrb])
        for tt in range(NTT):
            t0 = tt * TT
            for c in range(NC):
                em.dma("sync", xst[:, c * TT:(c + 1) * TT], xown[c][:, t0:t0 + TT],
                       writes=[ctx.bufs["xst"]])
            em.op("vector", lambda e, o=xbt[:, :], i=xst[:, :]: e.tensor_copy(o, i),
                  reads=[ctx.bufs["xst"]], writes=[ctx.bufs["xbt"]])
            def evq(nout0, tt_, pt, pb, tt=tt):
                oc = nout0 // 128
                em.op("scalar", lambda e, o=qbf[:, oc * T + tt * TT:oc * T + tt * TT + TT],
                      i=pt[0:128, 0:TT], b=bqt[:, oc:oc + 1]:
                      e.activation(o, i, AF.Identity, bias=b, scale=0.125),
                      reads=[pb, ctx.bufs["bqt"]], writes=[qb])
            emit_gemm(ctx, x_tile=xbt, x_buf=ctx.bufs["xbt"], x_col0=0, Kdim=D,
                      w_dram_ap=wq, w_buf=em.buf(), Ndim=D, evac=evq, slab_n=256,
                      tts=[0], psum_ids=(0, 1), tile_T=TT, gst=gst, w_f32=True)

        if DEBUG_L2:
            dvd = nc.dram_tensor("dbg_vdr", [128, D], F32, kind="ExternalOutput").ap()
            vstg = ctx.tiles["tmp"]
            em.dma("gpsimd", vstg[:, 0:1024].rearrange("p (a b) -> p a b", b=512),
                   vdr[12].rearrange("p (a b) -> p a b", b=512), deps=[Tok("gpsimd", ctx.em.cnt["gpsimd"])],
                   writes=[ctx.bufs["tmp"][0], ctx.bufs["tmp"][1]])
            em.op("vector", lambda e, o=ctx.tiles["sq"][:, 0:1024], i=vstg[:, 0:1024]:
                  e.tensor_copy(o, i), reads=[ctx.bufs["tmp"][0]], writes=[ctx.bufs["sq"][0]])
            em.dma("gpsimd", dvd, ctx.tiles["sq"][:, 0:1024], reads=[ctx.bufs["sq"][0]])
        nkc1 = SFULL // TT
        nkc2 = SFULL // 128
        for grp in range(NTT):
            g0 = grp * TT
            for h in range(16):
                hc, hr = h // 2, (h % 2) * 64
                em.op("scalar", lambda e, o=qaug[0:64, h * TT:(h + 1) * TT],
                      i=qbf[hr:hr + 64, hc * T + g0:hc * T + g0 + TT]:
                      e.copy(o, i), reads=[qb], writes=[qaugb])
            for h in range(16):
                hc, hr = h // 2, (h % 2) * 64
                par = h % 2
                em.dma("sync", ktile[0:64, par * SFULL:(par + 1) * SFULL],
                       kdr[hc][hr:hr + 64, :], writes=[ctx.bufs["ktile"][par]],
                       deps=[])
                for sub in range(4):
                    cmx = ctx.tiles["onorm"]
                    for kc in range(nkc1):
                        pt, pb = ctx.psum[4 + kc % 2]
                        em.op("tensor", lambda e, o=pt[0:128, 0:TT],
                              l=qaug[0:64, h * TT + sub * 128:h * TT + sub * 128 + 128],
                              r=ktile[0:64, par * SFULL + kc * TT:par * SFULL + kc * TT + TT]:
                              e.matmul(o, l, r, start=True, stop=True),
                              reads=[qaugb, ctx.bufs["ktile"][par]], writes=[pb])
                        em.op("vector", lambda e, o=cmx[:, kc:kc + 1], i=pt[0:128, 0:TT]:
                              e.reduce_max(o, i, axis=AX.X),
                              reads=[pb], writes=[ctx.bufs["onorm"]])
                    em.op("vector", lambda e, o=m_all[:, h * 4 + sub:h * 4 + sub + 1],
                          i=cmx[:, 0:nkc1]: e.reduce_max(o, i, axis=AX.X),
                          reads=[ctx.bufs["onorm"]], writes=[mab])
            tp, tpb = ctx.psum[6]
            em.op("tensor", lambda e, o=tp[0:64, 0:128], i=m_all[:, 0:64], idt=ctx.ident:
                  e.transpose(o, i, idt), reads=[mab, cb], writes=[tpb])
            em.op("vector", lambda e, o=mT[:, :], i=tp[0:64, 0:128]: e.tensor_copy(o, i),
                  reads=[tpb], writes=[mTb])
            em.dma("gpsimd", mdr, mT[:, :], reads=[mTb], writes=[mdrb])
            for h in range(16):
                em.dma("gpsimd", qaug[64:65, h * TT:(h + 1) * TT],
                       mdr[4 * h:4 * h + 4, :].rearrange("a b -> (a b)")[None, :],
                       reads=[mdrb], writes=[qaugb])
            for h in range(16):
                hc, hr = h // 2, (h % 2) * 64
                par = h % 2
                em.dma("sync", ktile[0:64, par * SFULL:(par + 1) * SFULL],
                       kdr[hc][hr:hr + 64, :], writes=[ctx.bufs["ktile"][par]])
                vdst = vaug[:, par * 4160:(par + 1) * 4160].rearrange(
                    "p (s e) -> p s e", e=65)[:, :, 0:64]
                em.dma("sync", vdst, vdr[:, :, h * 64:(h + 1) * 64].rearrange("s p d -> p s d"),
                       writes=[ctx.bufs["vaug"][par]])
                osb = ctx.tiles["osb"]
                osbb = ctx.bufs["osb"]
                for burst in range(8):
                    for j in range(8):
                        kc = burst * 8 + j
                        spt, spb = ctx.psum[4 + j % 2]
                        em.op("tensor", lambda e, o=spt[0:128, 0:TT],
                              l=ktile[0:65, par * SFULL + kc * 128:par * SFULL + kc * 128 + 128],
                              r=qaug[0:65, h * TT:(h + 1) * TT]:
                              e.matmul(o, l, r, start=True, stop=True),
                              reads=[ctx.bufs["ktile"][par], qaugb], writes=[spb])
                        em.op("scalar", lambda e, o=ptsb[:, j * TT:(j + 1) * TT],
                              i=spt[0:128, 0:TT]: e.activation(o, i, AF.Exp),
                              reads=[spb], writes=[ctx.bufs["ptsb"][j]])
                    ob, obb = ctx.psum[2]

                    def mmpv(e, burst=burst, par=par):
                        last = None
                        for j in range(8):
                            kc = burst * 8 + j
                            last = e.matmul(ob[0:65, 0:TT],
                                            vaug[:, par * 4160 + kc * 65:par * 4160 + (kc + 1) * 65],
                                            ptsb[:, j * TT:(j + 1) * TT],
                                            start=(j == 0), stop=(j == 7))
                        return last
                    em.op("tensor", mmpv,
                          reads=[ctx.bufs["vaug"][par]] + [ctx.bufs["ptsb"][j] for j in range(8)],
                          writes=[obb])
                    if burst == 0:
                        em.op("vector", lambda e, o=osb[0:65, 0:TT], i=ob[0:65, 0:TT]:
                              e.tensor_copy(o, i), reads=[obb], writes=[osbb])
                    else:
                        em.op("vector", lambda e, o=osb[0:65, 0:TT], i0=osb[0:65, 0:TT],
                              i1=ob[0:65, 0:TT]: e.tensor_add(o, i0, i1),
                              reads=[obb, osbb], writes=[osbb])
                em.op("vector", lambda e, o=rden[0:1, :], i=osb[64:65, 0:TT]:
                      e.reciprocal(o, i), reads=[osbb], writes=[ctx.bufs["rden"]])
                bcp, bcpb = ctx.psum[3]
                em.op("tensor", lambda e, o=bcp[0:64, 0:TT], r=rden[0:1, :]:
                      e.matmul(o, ctx.ones_row[0:1, 0:64], r, start=True, stop=True),
                      reads=[ctx.bufs["rden"], cb], writes=[bcpb])
                em.op("vector", lambda e,
                      o=afm[hr:hr + 64, hc * TT:(hc + 1) * TT],
                      i0=osb[0:64, 0:TT],
                      i1=bcp[0:64, 0:TT]: e.tensor_mul(o, i0, i1),
                      reads=[osbb, bcpb], writes=[ab])
            for c in range(NC):
                em.dma("sync", accT[:, c * TT:(c + 1) * TT], xown[c][:, g0:g0 + TT],
                       writes=[accTb])
            def evo(nout0, tt_, pt, pb):
                oc = nout0 // 128
                dst = accT[:, oc * TT:(oc + 1) * TT]
                tpar = oc % 2
                tsl = tmp[:, tpar * TT:(tpar + 1) * TT]
                em.op("vector", lambda e, o=tsl, i=pt[0:128, 0:TT], b=bot[:, oc:oc + 1]:
                      e.tensor_scalar(o, i, b, None, ALU.add),
                      reads=[pb, ctx.bufs["bot"]], writes=[ctx.bufs["tmp"][tpar]])
                em.op("vector", lambda e, o=dst, i0=dst, i1=tsl:
                      e.tensor_add(o, i0, i1), reads=[ctx.bufs["tmp"][tpar], accTb],
                      writes=[accTb])
            emit_gemm(ctx, x_tile=afm, x_buf=ab, x_col0=0, Kdim=D, w_dram_ap=wo,
                      w_buf=em.buf(), Ndim=D, evac=evo, slab_n=256, tts=[0],
                      psum_ids=(0, 1), tile_T=TT, gst=gst, w_f32=True)
            emit_ln_tile(ctx, accT, accTb, yout, g0)
        if DEBUG_L2:
            dafm = nc.dram_tensor("dbg_afm", [128, NC * TT], BF16, kind="ExternalOutput").ap()
            em.dma("gpsimd", dafm, afm[:, :], reads=[ab])
            dqa = nc.dram_tensor("dbg_qaug", [65, 16 * TT], BF16, kind="ExternalOutput").ap()
            em.dma("gpsimd", dqa, qaug[:, :], reads=[qaugb])
        em.finalize()
    return nc


DEBUG_L2 = False


def prep_l2_attn(inputs, x):
    wqkv = np.asarray(inputs["L2_attn_Wqkv"], np.float32)
    bqkv = np.asarray(inputs["L2_attn_bqkv"], np.float32)
    wq = np.ascontiguousarray(wqkv[0:1024, :].T)
    wk = np.ascontiguousarray(wqkv[1024:2048, :].T)
    wv = np.ascontiguousarray(wqkv[2048:3072, :].T)
    full = [fm(x[b]) for b in range(2)]
    shards = shard_x(x)
    base = {
        "wq": wq, "wk": wk, "wv": wv,
        "wo": np.ascontiguousarray(np.asarray(inputs["L2_attn_Wo"], np.float32).T),
        "bq": chunk_vec(bqkv[0:1024]),
        "bk": chunk_vec(bqkv[1024:2048]),
        "bvr": np.ascontiguousarray(np.broadcast_to(bqkv[2048:3072], (128, D))),
        "bo": chunk_vec(np.asarray(inputs["L2_attn_bo"], np.float32)),
        "consts": host_consts(),
    }
    return [dict(base, xfull=full[i // 4], xown=shards[i]) for i in range(8)]


def stage_l2_attn(inputs, x):
    res = run_stage(build_l2_attn, prep_l2_attn(inputs, x))
    return unshard_x([np.asarray(r["yout"]) for r in res])





def build_l0_attn():
    nc = bass.Bass("TRN2", target_bir_lowering=False, debug=False)
    xfull = nc.dram_tensor("xfull", [NC, 128, SFULL], F32, kind="ExternalInput").ap()
    xq = nc.dram_tensor("xq", [NC, 128, T], F32, kind="ExternalInput").ap()
    consts = nc.dram_tensor("consts", [128, CONSTW], F32, kind="ExternalInput").ap()
    yout = nc.dram_tensor("yout", [NC, 128, T], F32, kind="ExternalOutput").ap()
    wq = nc.dram_tensor("wq", [D, D], F32, kind="ExternalInput").ap()
    wk = nc.dram_tensor("wk", [D, D], F32, kind="ExternalInput").ap()
    wv = nc.dram_tensor("wv", [D, D], F32, kind="ExternalInput").ap()
    wo = nc.dram_tensor("wo", [D, D], F32, kind="ExternalInput").ap()
    bq = nc.dram_tensor("bq", [128, 8], F32, kind="ExternalInput").ap()
    bk = nc.dram_tensor("bk", [128, 8], F32, kind="ExternalInput").ap()
    bvr = nc.dram_tensor("bvr", [128, D], F32, kind="ExternalInput").ap()
    bo = nc.dram_tensor("bo", [128, 8], F32, kind="ExternalInput").ap()
    bias1 = nc.dram_tensor("bias1", [128, 512], F32, kind="ExternalInput").ap()
    bias2 = nc.dram_tensor("bias2", [16, 128, 512], F32, kind="ExternalInput").ap()
    kdr = nc.dram_tensor("kdr", [NC, 128, SFULL], BF16).ap()
    vdr = nc.dram_tensor("vdr", [SFULL // 128, 128, D], BF16).ap()
    mdr = nc.dram_tensor("mdr", [64, 128], F32).ap()

    with contextlib.ExitStack() as stack:
        ctx = Ctx(nc, stack)
        em = ctx.em
        xbt = ctx.sbuf("xbt", [128, NC * TT], BF16)
        qbf = ctx.sbuf("qbf", [128, NC * T], BF16)
        ctx.sbuf("wsA", [128, 2048], BF16)
        ctx.sbuf("wsB", [128, 2048], BF16)
        ctx.sbuf("wgA", [128, 2048], F32)
        ctx.sbuf("wgB", [128, 2048], F32)
        kst = ctx.sbuf("kst", [128, 2 * TT], BF16, nbuf=2)
        vst = ctx.sbuf("vst", [128, 2 * TT], BF16, nbuf=2)
        ktile = ctx.sbuf("ktile", [65, 2 * SFULL], BF16, nbuf=2)
        vaug = ctx.sbuf("vaug", [128, 2 * 64 * 65], BF16, nbuf=2)
        qaug = ctx.sbuf("qaug", [65, 16 * TT], BF16)
        ptsb = ctx.sbuf("ptsb", [128, 8 * TT], BF16, nbuf=8)
        osb = ctx.sbuf("osb", [65, TT], F32)
        m_all = ctx.sbuf("m_all", [128, 64], F32)
        mT = ctx.sbuf("mT", [64, 128], F32)
        rden = ctx.sbuf("rden", [1, TT], F32)
        onorm = ctx.sbuf("onorm", [128, 64], F32)
        afm = ctx.sbuf("afm", [128, NC * TT], BF16)
        bvt = ctx.sbuf("bvt", [128, D], F32)
        b1t = ctx.sbuf("b1t", [128, 512], F32)
        b2t = ctx.sbuf("b2t", [128, 512], F32)
        tmp = ctx.sbuf("tmp", [128, 2 * TT], F32, nbuf=2)
        sq = ctx.sbuf("sq", [128, 2 * TT], F32, nbuf=2)
        row = ctx.sbuf("row", [1, 3 * TT], F32)
        accT = ctx.sbuf("accT", [128, NC * TT], F32)
        bqt = ctx.sbuf("bqt", [128, 8], F32)
        bkt = ctx.sbuf("bkt", [128, 8], F32)
        bot = ctx.sbuf("bot", [128, 8], F32)
        ctx.alloc_psum(8)
        ctx.load_consts(consts)
        cb = ctx.cb
        qb = ctx.bufs["qbf"]; ab = ctx.bufs["afm"]; accTb = ctx.bufs["accT"]
        kdrb = em.buf("kdr"); vdrb = em.buf("vdr"); mdrb = em.buf("mdr")
        qaugb = ctx.bufs["qaug"]; mab = ctx.bufs["m_all"]; mTb = ctx.bufs["mT"]
        osbb = ctx.bufs["osb"]

        for nm, ap_ in (("bqt", bq), ("bkt", bk), ("bot", bo), ("bvt", bvr)):
            em.dma("sync", ctx.tiles[nm][:, :], ap_, writes=[ctx.bufs[nm]])
        em.dma("sync", b1t[:, :], bias1, writes=[ctx.bufs["b1t"]])
        em.op("vector", lambda e, o=bqt[:, :], i=bqt[:, :]: e.tensor_scalar_mul(o, i, 0.125),
              reads=[ctx.bufs["bqt"]], writes=[ctx.bufs["bqt"]])
        for par in range(2):
            em.op("vector", lambda e, o=ktile[64:65, par * SFULL:(par + 1) * SFULL]:
                  e.memset(o, -1.0), writes=[ctx.bufs["ktile"][par]])
            ones_view = vaug[:, par * 4160:(par + 1) * 4160].rearrange(
                "p (s e) -> p s e", e=65)[:, :, 64:65]
            em.op("vector", lambda e, o=ones_view: e.memset(o, 1.0),
                  writes=[ctx.bufs["vaug"][par]])

        gst = GemmState()
        for ft in range(SFULL // TT):
            f0 = ft * TT
            for c in range(NC):
                em.dma("gpsimd", xbt[:, c * TT:(c + 1) * TT], xfull[c][:, f0:f0 + TT],
                       writes=[ctx.bufs["xbt"]])
            def evk(nout0, tt_, pt, pb, ft=ft):
                oc = nout0 // 128
                kpar = oc % 2
                em.op("scalar", lambda e, o=kst[:, kpar * TT:(kpar + 1) * TT],
                      i=pt[0:128, 0:TT], b=bkt[:, oc:oc + 1]:
                      e.activation(o, i, AF.Identity, bias=b),
                      reads=[pb, ctx.bufs["bkt"]], writes=[ctx.bufs["kst"][kpar]])
                em.dma("gpsimd", kdr[oc][:, ft * TT:(ft + 1) * TT],
                       kst[:, kpar * TT:(kpar + 1) * TT],
                       reads=[ctx.bufs["kst"][kpar]], writes=[kdrb])
            emit_gemm(ctx, x_tile=xbt, x_buf=ctx.bufs["xbt"], x_col0=0, Kdim=D,
                      w_dram_ap=wk, w_buf=em.buf(), Ndim=D, evac=evk, slab_n=256,
                      tts=[0], psum_ids=(0, 1), tile_T=TT, gst=gst, w_f32=True)
            for vs in range(4):
                n0 = vs * 256
                w, wb = ((ctx.tiles["wsA"], ctx.bufs["wsA"]) if vs % 2 == 0
                         else (ctx.tiles["wsB"], ctx.bufs["wsB"]))
                g, gb = ((ctx.tiles["wgA"], ctx.bufs["wgA"]) if vs % 2 == 0
                         else (ctx.tiles["wgB"], ctx.bufs["wgB"]))
                em.dma("sync", g[:, 0:NC * 256].rearrange("p (c n) -> p c n", n=256),
                       wv[:, n0:n0 + 256].rearrange("(c p) n -> p c n", p=128),
                       writes=[gb])
                em.op("scalar", lambda e, o=w[:, 0:NC * 256], i=g[:, 0:NC * 256]:
                      e.copy(o, i), reads=[gb], writes=[wb])
                for st_ in range(4):
                    pt, pb = ctx.psum[2 + st_ % 2]
                    def mmv(e, st_=st_, w=w, pt=pt):
                        last = None
                        for kc in range(NC):
                            last = e.matmul(pt[0:128, 0:256],
                                            xbt[:, kc * TT + st_ * 128:kc * TT + st_ * 128 + 128],
                                            w[:, kc * 256:(kc + 1) * 256],
                                            start=(kc == 0), stop=(kc == NC - 1))
                        return last
                    em.op("tensor", mmv, reads=[wb, ctx.bufs["xbt"]], writes=[pb])
                    vpar = st_ % 2
                    em.op("vector", lambda e, o=vst[:, vpar * TT:vpar * TT + 256],
                          i0=pt[0:128, 0:256], i1=bvt[:, n0:n0 + 256]:
                          e.tensor_add(o, i0, i1), reads=[pb, ctx.bufs["bvt"]],
                          writes=[ctx.bufs["vst"][vpar]])
                    em.dma("gpsimd", vdr[ft * 4 + st_][:, n0:n0 + 256],
                           vst[:, vpar * TT:vpar * TT + 256],
                           reads=[ctx.bufs["vst"][vpar]], writes=[vdrb])
        for tt in range(NTT):
            t0 = tt * TT
            for c in range(NC):
                em.dma("gpsimd", xbt[:, c * TT:(c + 1) * TT], xq[c][:, t0:t0 + TT],
                       writes=[ctx.bufs["xbt"]])
            def evq(nout0, tt_, pt, pb, tt=tt):
                oc = nout0 // 128
                em.op("scalar", lambda e, o=qbf[:, oc * T + tt * TT:oc * T + tt * TT + TT],
                      i=pt[0:128, 0:TT], b=bqt[:, oc:oc + 1]:
                      e.activation(o, i, AF.Identity, bias=b, scale=0.125),
                      reads=[pb, ctx.bufs["bqt"]], writes=[qb])
            emit_gemm(ctx, x_tile=xbt, x_buf=ctx.bufs["xbt"], x_col0=0, Kdim=D,
                      w_dram_ap=wq, w_buf=em.buf(), Ndim=D, evac=evq, slab_n=256,
                      tts=[0], psum_ids=(0, 1), tile_T=TT, gst=gst, w_f32=True)

        for grp in range(NTT):
            g0 = grp * TT
            nk2 = 16 * (grp + 1)
            for h in range(16):
                hc, hr = h // 2, (h % 2) * 64
                em.op("scalar", lambda e, o=qaug[0:64, h * TT:(h + 1) * TT],
                      i=qbf[hr:hr + 64, hc * T + g0:hc * T + g0 + TT]:
                      e.copy(o, i), reads=[qb], writes=[qaugb])
            for h in range(16):
                hc, hr = h // 2, (h % 2) * 64
                par = h % 2
                em.dma("sync", ktile[0:64, par * SFULL:(par + 1) * SFULL],
                       kdr[hc][hr:hr + 64, :], writes=[ctx.bufs["ktile"][par]])
                for sub in range(4):
                    s = 4 * grp + sub
                    cmx = ctx.tiles["onorm"]
                    for kc in range(s + 1):
                        pt, pb = ctx.psum[4 + kc % 2]
                        em.op("tensor", lambda e, o=pt[0:128, 0:TT],
                              l=qaug[0:64, h * TT + sub * 128:h * TT + sub * 128 + 128],
                              r=ktile[0:64, par * SFULL + kc * TT:par * SFULL + kc * TT + TT]:
                              e.matmul(o, l, r, start=True, stop=True),
                              reads=[qaugb, ctx.bufs["ktile"][par]], writes=[pb])
                        if kc == s:
                            tpar = sub % 2
                            em.op("vector", lambda e, o=tmp[:, tpar * TT:(tpar + 1) * TT],
                                  i0=pt[0:128, 0:TT], i1=b1t[:, :]:
                                  e.tensor_add(o, i0, i1),
                                  reads=[pb, ctx.bufs["b1t"]], writes=[ctx.bufs["tmp"][tpar]])
                            em.op("vector", lambda e, o=cmx[:, kc:kc + 1],
                                  i=tmp[:, tpar * TT:(tpar + 1) * TT]:
                                  e.reduce_max(o, i, axis=AX.X),
                                  reads=[ctx.bufs["tmp"][tpar]], writes=[ctx.bufs["onorm"]])
                        else:
                            em.op("vector", lambda e, o=cmx[:, kc:kc + 1], i=pt[0:128, 0:TT]:
                                  e.reduce_max(o, i, axis=AX.X),
                                  reads=[pb], writes=[ctx.bufs["onorm"]])
                    em.op("vector", lambda e, o=m_all[:, h * 4 + sub:h * 4 + sub + 1],
                          i=cmx[:, 0:s + 1]: e.reduce_max(o, i, axis=AX.X),
                          reads=[ctx.bufs["onorm"]], writes=[mab])
            tp, tpb = ctx.psum[6]
            em.op("tensor", lambda e, o=tp[0:64, 0:128], i=m_all[:, 0:64], idt=ctx.ident:
                  e.transpose(o, i, idt), reads=[mab, cb], writes=[tpb])
            em.op("vector", lambda e, o=mT[:, :], i=tp[0:64, 0:128]: e.tensor_copy(o, i),
                  reads=[tpb], writes=[mTb])
            em.dma("gpsimd", mdr, mT[:, :], reads=[mTb], writes=[mdrb])
            for h in range(16):
                em.dma("gpsimd", qaug[64:65, h * TT:(h + 1) * TT],
                       mdr[4 * h:4 * h + 4, :].rearrange("a b -> (a b)")[None, :],
                       reads=[mdrb], writes=[qaugb])
            for h in range(16):
                hc, hr = h // 2, (h % 2) * 64
                par = h % 2
                em.dma("sync", ktile[0:64, par * SFULL:(par + 1) * SFULL],
                       kdr[hc][hr:hr + 64, :], writes=[ctx.bufs["ktile"][par]])
                vdst = vaug[:, par * 4160:(par + 1) * 4160].rearrange(
                    "p (s e) -> p s e", e=65)[:, :, 0:64]
                em.dma("sync", vdst, vdr[:, :, h * 64:(h + 1) * 64].rearrange("s p d -> p s d"),
                       writes=[ctx.bufs["vaug"][par]])
                for burst in range((nk2 + 7) // 8):
                    j0 = burst * 8
                    jn = min(8, nk2 - j0)
                    for j in range(jn):
                        kc = j0 + j
                        spt, spb = ctx.psum[4 + j % 2]
                        em.op("tensor", lambda e, o=spt[0:128, 0:TT],
                              l=ktile[0:65, par * SFULL + kc * 128:par * SFULL + kc * 128 + 128],
                              r=qaug[0:65, h * TT:(h + 1) * TT]:
                              e.matmul(o, l, r, start=True, stop=True),
                              reads=[ctx.bufs["ktile"][par], qaugb], writes=[spb])
                        r_ = kc - 16 * grp
                        if r_ >= 0:
                            em.dma("sync", b2t[:, 0:512], bias2[r_],
                                   writes=[ctx.bufs["b2t"]])
                            em.op("vector", lambda e, o=sq[:, (j % 2) * TT:(j % 2) * TT + TT],
                                  i0=spt[0:128, 0:TT], i1=b2t[:, 0:512]:
                                  e.tensor_add(o, i0, i1),
                                  reads=[spb, ctx.bufs["b2t"]],
                                  writes=[ctx.bufs["sq"][j % 2]])
                            em.op("scalar", lambda e, o=ptsb[:, j * TT:(j + 1) * TT],
                                  i=sq[:, (j % 2) * TT:(j % 2) * TT + TT]:
                                  e.activation(o, i, AF.Exp),
                                  reads=[ctx.bufs["sq"][j % 2]], writes=[ctx.bufs["ptsb"][j]])
                        else:
                            em.op("scalar", lambda e, o=ptsb[:, j * TT:(j + 1) * TT],
                                  i=spt[0:128, 0:TT]: e.activation(o, i, AF.Exp),
                                  reads=[spb], writes=[ctx.bufs["ptsb"][j]])
                    ob, obb = ctx.psum[2]
                    def mmpv(e, j0=j0, jn=jn, par=par):
                        last = None
                        for j in range(jn):
                            kc = j0 + j
                            last = e.matmul(ob[0:65, 0:TT],
                                            vaug[:, par * 4160 + kc * 65:par * 4160 + (kc + 1) * 65],
                                            ptsb[:, j * TT:(j + 1) * TT],
                                            start=(j == 0), stop=(j == jn - 1))
                        return last
                    em.op("tensor", mmpv,
                          reads=[ctx.bufs["vaug"][par]] + [ctx.bufs["ptsb"][j] for j in range(jn)],
                          writes=[obb])
                    if burst == 0:
                        em.op("vector", lambda e, o=osb[0:65, 0:TT], i=ob[0:65, 0:TT]:
                              e.tensor_copy(o, i), reads=[obb], writes=[osbb])
                    else:
                        em.op("vector", lambda e, o=osb[0:65, 0:TT], i0=osb[0:65, 0:TT],
                              i1=ob[0:65, 0:TT]: e.tensor_add(o, i0, i1),
                              reads=[obb, osbb], writes=[osbb])
                em.op("vector", lambda e, o=rden[0:1, :], i=osb[64:65, 0:TT]:
                      e.reciprocal(o, i), reads=[osbb], writes=[ctx.bufs["rden"]])
                bcp, bcpb = ctx.psum[3]
                em.op("tensor", lambda e, o=bcp[0:64, 0:TT], r=rden[0:1, :]:
                      e.matmul(o, ctx.ones_row[0:1, 0:64], r, start=True, stop=True),
                      reads=[ctx.bufs["rden"], cb], writes=[bcpb])
                em.op("vector", lambda e,
                      o=afm[hr:hr + 64, hc * TT:(hc + 1) * TT],
                      i0=osb[0:64, 0:TT],
                      i1=bcp[0:64, 0:TT]: e.tensor_mul(o, i0, i1),
                      reads=[osbb, bcpb], writes=[ab])
            for c in range(NC):
                em.dma("sync", accT[:, c * TT:(c + 1) * TT], xq[c][:, g0:g0 + TT],
                       writes=[accTb])
            def evo(nout0, tt_, pt, pb):
                oc = nout0 // 128
                dst = accT[:, oc * TT:(oc + 1) * TT]
                tpar = oc % 2
                tsl = tmp[:, tpar * TT:(tpar + 1) * TT]
                em.op("vector", lambda e, o=tsl, i=pt[0:128, 0:TT], b=bot[:, oc:oc + 1]:
                      e.tensor_scalar(o, i, b, None, ALU.add),
                      reads=[pb, ctx.bufs["bot"]], writes=[ctx.bufs["tmp"][tpar]])
                em.op("vector", lambda e, o=dst, i0=dst, i1=tsl:
                      e.tensor_add(o, i0, i1), reads=[ctx.bufs["tmp"][tpar], accTb],
                      writes=[accTb])
            emit_gemm(ctx, x_tile=afm, x_buf=ab, x_col0=0, Kdim=D, w_dram_ap=wo,
                      w_buf=em.buf(), Ndim=D, evac=evo, slab_n=256, tts=[0],
                      psum_ids=(0, 1), tile_T=TT, gst=gst, w_f32=True)
            emit_ln_tile(ctx, accT, accTb, yout, g0)
        em.finalize()
    return nc


def l0_tiles(g):
    return [4 * s + g for s in range(16)]


def prep_l0_attn(inputs, x):
    qkv_w = np.asarray(inputs["L0_attn_qkv_w"], np.float32)
    qkv_b = np.asarray(inputs["L0_attn_qkv_b"], np.float32)
    out_w = np.asarray(inputs["L0_attn_out_w"], np.float32)
    out_b = np.asarray(inputs["L0_attn_out_b"], np.float32)
    wq = np.ascontiguousarray(qkv_w[:, :, 0:64].reshape(D, D))
    wk = np.ascontiguousarray(qkv_w[:, :, 64:128].reshape(D, D))
    wv = np.ascontiguousarray(qkv_w[:, :, 128:192].reshape(D, D))
    bq = chunk_vec(np.ascontiguousarray(qkv_b[:, 0:64].reshape(D)))
    bk = chunk_vec(np.ascontiguousarray(qkv_b[:, 64:128].reshape(D)))
    bvr = np.ascontiguousarray(np.broadcast_to(
        np.ascontiguousarray(qkv_b[:, 128:192].reshape(D)), (128, D)))
    wo = np.ascontiguousarray(out_w.reshape(D, D))
    full = [fm(x[b]) for b in range(2)]
    maps = []
    for core in range(8):
        b, g = core // 4, core % 4
        rows = np.concatenate([np.arange(t * 128, (t + 1) * 128) for t in l0_tiles(g)])
        xq = fm(x[b][rows, :])
        i_loc = np.arange(128)[:, None]
        j = np.arange(512)[None, :]
        b1 = np.where(j <= 128 * g + i_loc, 0.0, -3e38).astype(np.float32)
        b2 = np.zeros((16, 128, 512), np.float32)
        for r in range(16):
            for s2 in range(4):
                rel = r - (4 * s2 + g)
                p_ = np.arange(128)[:, None]
                i_ = np.arange(128)[None, :]
                if rel < 0:
                    m = np.zeros((128, 128), np.float32)
                else:
                    m = np.where(128 * rel + p_ > i_, -3e38, 0.0).astype(np.float32)
                b2[r][:, s2 * 128:(s2 + 1) * 128] = m
        maps.append({
            "xfull": full[b], "xq": xq, "consts": host_consts(),
            "wq": wq, "wk": wk, "wv": wv, "wo": wo,
            "bq": bq, "bk": bk, "bvr": bvr, "bo": chunk_vec(out_b),
            "bias1": b1, "bias2": b2,
        })
    return maps


def stage_l0_attn(inputs, x):
    res = run_stage(build_l0_attn, prep_l0_attn(inputs, x))
    out = np.empty((2, 8192, D), np.float32)
    for core in range(8):
        b, g = core // 4, core % 4
        yo = unfm(np.asarray(res[core]["yout"]))
        for s, t in enumerate(l0_tiles(g)):
            out[b, t * 128:(t + 1) * 128, :] = yo[s * 128:(s + 1) * 128, :]
    return out





DEBUG_L0 = False
HALO0 = 640
XT0 = T + HALO0
KT0 = T + 512
QT0 = T + 128
NOC1 = 22


def build_l0_ffn(final=False):
    nc = bass.Bass("TRN2", target_bir_lowering=False, debug=False)
    xin = nc.dram_tensor("xin", [NC, 128, XT0], F32, kind="ExternalInput").ap()
    consts = nc.dram_tensor("consts", [128, CONSTW], F32, kind="ExternalInput").ap()
    yout = nc.dram_tensor("yout", [NC, 128, T], F32, kind="ExternalOutput").ap()
    wq = nc.dram_tensor("wq", [D, D], F32, kind="ExternalInput").ap()
    wk = nc.dram_tensor("wk", [D, D], F32, kind="ExternalInput").ap()
    wv = nc.dram_tensor("wv", [D, D], F32, kind="ExternalInput").ap()
    wo = nc.dram_tensor("wo", [D, D], F32, kind="ExternalInput").ap()
    w1 = nc.dram_tensor("w1", [D, 2 * NOC1 * 128], F32, kind="ExternalInput").ap()
    w2 = nc.dram_tensor("w2", [NOC1 * 128, D], F32, kind="ExternalInput").ap()
    b1a = nc.dram_tensor("b1a", [128, NOC1], F32, kind="ExternalInput").ap()
    b1g = nc.dram_tensor("b1g", [128, NOC1], F32, kind="ExternalInput").ap()
    b2 = nc.dram_tensor("b2", [128, 8], F32, kind="ExternalInput").ap()
    lng = nc.dram_tensor("lng", [128, 8], F32, kind="ExternalInput").ap()
    lnb = nc.dram_tensor("lnb", [128, 8], F32, kind="ExternalInput").ap()
    qs_r = nc.dram_tensor("qs_r", [128, 64], F32, kind="ExternalInput").ap()
    qs_sw = nc.dram_tensor("qs_sw", [128, 64], F32, kind="ExternalInput").ap()
    ks_r = nc.dram_tensor("ks_r", [128, 64], F32, kind="ExternalInput").ap()
    ks_sw = nc.dram_tensor("ks_sw", [128, 64], F32, kind="ExternalInput").ap()
    t1q = nc.dram_tensor("t1q", [17, 128, 64], F32, kind="ExternalInput").ap()
    t2q = nc.dram_tensor("t2q", [17, 128, 64], F32, kind="ExternalInput").ap()
    t1k = nc.dram_tensor("t1k", [20, 128, 64], F32, kind="ExternalInput").ap()
    t2k = nc.dram_tensor("t2k", [20, 128, 64], F32, kind="ExternalInput").ap()
    biasA = nc.dram_tensor("biasA", [17, 128, 384], F32, kind="ExternalInput").ap()
    qmask0 = nc.dram_tensor("qmask0", [128, 128], F32, kind="ExternalInput").ap()
    if final:
        xorig = nc.dram_tensor("xorig", [NC, 128, T], F32, kind="ExternalInput").ap()
    x1dr = nc.dram_tensor("x1dr", [NC, 128, QT0], F32,
                          kind="ExternalOutput" if DEBUG_L0 else "Internal").ap()
    w2bf = nc.dram_tensor("w2bf", [NOC1 * 128, D], BF16).ap()

    SH = [0, 0, 0, 0, -1, -1, -1, -1]

    with contextlib.ExitStack() as stack:
        ctx = Ctx(nc, stack)
        em = ctx.em
        arena = ctx.sbuf("arena", [128, 40 * 1024], BF16, nbuf=3)
        ab_kf, ab_vt, ab_u = ctx.bufs["arena"]
        KF0, VT0 = 0, 20480
        U0, HA0 = 0, NOC1 * 512
        wcur = ctx.sbuf("wcur", [128, NC * 1024], BF16)
        wcur2 = ctx.sbuf("wcur2", [128, NC * 1024], BF16)
        ctx.sbuf("wsA", [128, 2816], BF16)
        ctx.tiles["wsB"] = ctx.tiles["wsA"]
        ctx.bufs["wsB"] = ctx.bufs["wsA"]
        ctx.sbuf("wgA", [128, 2048], F32)
        ctx.tiles["wgB"] = ctx.tiles["wgA"]
        ctx.bufs["wgB"] = ctx.bufs["wgA"]
        xpt = ctx.sbuf("xpt", [128, NC * 132], BF16)
        ktm = ctx.sbuf("ktm", [128, 1024], F32)
        krot = ctx.sbuf("krot", [128, 1024], F32)
        kbf_t = ctx.sbuf("kbf_t", [128, 1024], BF16)
        qft = ctx.sbuf("qft", [128, NC * 128], BF16)
        tabs = ctx.sbuf("tabs", [128, 2 * 64], F32)
        scv = ctx.sbuf("scv", [128, 4 * 64], F32)
        nrm = ctx.sbuf("nrm", [128, 3 * 16], F32)
        psb = ctx.sbuf("psb", [128, 384], BF16)
        pt_sb = ctx.sbuf("pt_sb", [128, 3 * 128], BF16)
        mrow = ctx.sbuf("mrow", [128, 4], F32)
        afm = ctx.sbuf("afm", [128, NC * 128], BF16)
        biasA_t = ctx.sbuf("biasA_t", [128, 384], F32)
        x1s = ctx.sbuf("x1s", [128, NC * 513], F32)
        xn_t = ctx.sbuf("xn_t", [128, NC * TT], BF16)
        x2t = ctx.sbuf("x2t", [128, NC * TT], F32)
        xrs = ctx.sbuf("xrs", [128, NC * 128], F32)
        tmp = ctx.sbuf("tmp", [128, 2 * TT], F32, nbuf=2)
        sq = ctx.sbuf("sq", [128, 2 * TT], F32, nbuf=2)
        row = ctx.sbuf("row", [1, 3 * TT], F32)
        b1at = ctx.sbuf("b1at", [128, NOC1], F32)
        b1gt = ctx.sbuf("b1gt", [128, NOC1], F32)
        b2t = ctx.sbuf("b2t", [128, 8], F32)
        lngt = ctx.sbuf("lngt", [128, 8], F32)
        lnbt = ctx.sbuf("lnbt", [128, 8], F32)
        qm0 = ctx.sbuf("qm0", [128, 128], BF16)
        ctx.alloc_psum(7, 1)
        ctx.load_consts(consts)
        cb = ctx.cb
        wcb = ctx.bufs["wcur"]
        wcb2 = ctx.bufs["wcur2"]
        xpb = ctx.bufs["xpt"]
        x1b = em.buf("x1dr")

        for nm, ap_ in (("b1at", b1a), ("b1gt", b1g), ("b2t", b2), ("lngt", lng),
                        ("lnbt", lnb)):
            em.dma("sync", ctx.tiles[nm][:, :], ap_, writes=[ctx.bufs[nm]])
        scb = ctx.bufs["scv"]
        em.dma("sync", scv[:, 0:64], qs_r, writes=[scb])
        em.dma("sync", scv[:, 64:128], qs_sw, writes=[scb])
        em.dma("sync", scv[:, 128:192], ks_r, writes=[scb])
        em.dma("sync", scv[:, 192:256], ks_sw, writes=[scb])
        em.op("vector", lambda e, o=scv[:, 0:128], i=scv[:, 0:128]:
              e.tensor_scalar_mul(o, i, 8.0), reads=[scb], writes=[scb])
        em.dma("gpsimd", qm0[:, :], qmask0, writes=[ctx.bufs["qm0"]])

        def load_w(ap_, wt, wb):
            for qtr in range(4):
                g, gb = ctx.tiles["wgA"], ctx.bufs["wgA"]
                em.dma("sync", g[:, 0:2048].rearrange("p (c n) -> p c n", n=256),
                       ap_[:, qtr * 256:(qtr + 1) * 256].rearrange(
                           "(c p) n -> p c n", p=128), writes=[gb])
                for c in range(NC):
                    em.op("scalar", lambda e,
                          o=wt[:, c * 1024 + qtr * 256:c * 1024 + qtr * 256 + 256],
                          i=g[:, c * 256:(c + 1) * 256]:
                          e.copy(o, i), reads=[gb], writes=[wb])

        def load_xpt(s, base):
            for c in range(NC):
                em.dma("gpsimd", xpt[:, c * 132:c * 132 + 129],
                       xin[c][:, base + s * 128 - 1:base + s * 128 + 128],
                       writes=[xpb])

        def proj_tm(wt, wb, half, pt):
            def mmp(e, half=half, pt=pt, wt=wt):
                last = None
                for kc in range(NC):
                    col = kc * 132 + 1 + SH[kc]
                    last = e.matmul(pt[0:128, 0:512],
                                    xpt[:, col:col + 128],
                                    wt[:, kc * 1024 + half * 512:
                                       kc * 1024 + half * 512 + 512],
                                    start=(kc == 0), stop=(kc == NC - 1))
                return last
            return mmp

        def rotary_sub(s, tab1_ap, tab2_ap, sc_off, wt, wb, dst_view, dst_buf):
            tb = ctx.bufs["tabs"]
            for half in range(2):
                pt, pb = ctx.psum[2 + half]
                em.op("tensor", proj_tm(wt, wb, half, pt), reads=[wb, xpb], writes=[pb])
                em.op("vector", lambda e, o=ktm[:, half * 512:(half + 1) * 512],
                      i=pt[0:128, 0:512]: e.tensor_copy(o, i),
                      reads=[pb], writes=[ctx.bufs["ktm"]])
            em.dma("sync", tabs[:, 0:64], tab1_ap[s], writes=[tb])
            em.dma("sync", tabs[:, 64:128], tab2_ap[s], writes=[tb])
            em.op("vector", lambda e, o=tabs[:, 0:64], i0=tabs[:, 0:64],
                  i1=scv[:, sc_off:sc_off + 64]: e.tensor_mul(o, i0, i1),
                  reads=[tb, scb], writes=[tb])
            em.op("vector", lambda e, o=tabs[:, 64:128], i0=tabs[:, 64:128],
                  i1=scv[:, sc_off + 64:sc_off + 128]: e.tensor_mul(o, i0, i1),
                  reads=[tb, scb], writes=[tb])
            kb3 = ktm[:, :].rearrange("p (h d) -> p h d", d=64)
            em.op("scalar", lambda e, o=krot[:, 0:1024], i=ktm[:, :]:
                  e.activation(o, i, AF.Square),
                  reads=[ctx.bufs["ktm"]], writes=[ctx.bufs["krot"]])
            em.op("vector", lambda e, o=nrm[:, 0:16].rearrange("p (h u) -> p h u", u=1),
                  i=krot[:, 0:1024].rearrange("p (h d) -> p h d", d=64):
                  e.reduce_sum(o, i, axis=AX.X),
                  reads=[ctx.bufs["krot"]], writes=[ctx.bufs["nrm"]])
            em.op("scalar", lambda e, o=nrm[:, 16:32], i=nrm[:, 0:16]:
                  e.activation(o, i, AF.Sqrt), reads=[ctx.bufs["nrm"]],
                  writes=[ctx.bufs["nrm"]])
            em.op("vector", lambda e, o=nrm[:, 16:32], i=nrm[:, 16:32]:
                  e.tensor_scalar_max(o, i, 1e-12), reads=[ctx.bufs["nrm"]],
                  writes=[ctx.bufs["nrm"]])
            em.op("vector", lambda e, o=nrm[:, 32:48], i=nrm[:, 16:32]:
                  e.reciprocal(o, i), reads=[ctx.bufs["nrm"]], writes=[ctx.bufs["nrm"]])
            nb3 = nrm[:, 32:48].rearrange("p (h u) -> p h u", u=1).broadcast_to([128, 16, 64])
            em.op("vector", lambda e, o=krot[:, 0:1024].rearrange("p (h d) -> p h d", d=64),
                  i0=kb3, i1=nb3: e.tensor_tensor(o, i0, i1, ALU.mult),
                  reads=[ctx.bufs["ktm"], ctx.bufs["nrm"]], writes=[ctx.bufs["krot"]])
            t1b = tabs[:, 0:64].rearrange("p (u d) -> p u d", u=1).broadcast_to([128, 16, 64])
            qn3 = krot[:, 0:1024].rearrange("p (h d) -> p h d", d=64)
            qnsw = krot[:, 0:1024].rearrange("p (h s2 d) -> p h s2 d", s2=2, d=32)[:, :, ::-1, :]
            t2b4 = tabs[:, 64:128].rearrange("p (u s2 d) -> p u s2 d", u=1, s2=2,
                                              d=32).broadcast_to([128, 16, 2, 32])
            em.op("vector", lambda e, o=x2t[:, 0:1024].rearrange("p (h d) -> p h d", d=64),
                  i0=qn3, i1=t1b: e.tensor_tensor(o, i0, i1, ALU.mult),
                  reads=[ctx.bufs["krot"], tb], writes=[ctx.bufs["x2t"]])
            em.op("vector", lambda e,
                  o=ktm[:, :].rearrange("p (h s2 d) -> p h s2 d", s2=2, d=32),
                  i0=qnsw, i1=t2b4: e.tensor_tensor(o, i0, i1, ALU.mult),
                  reads=[ctx.bufs["krot"], tb], writes=[ctx.bufs["ktm"]])
            em.op("vector", lambda e, o=kbf_t[:, :], i0=x2t[:, 0:1024],
                  i1=ktm[:, :]: e.tensor_add(o, i0, i1),
                  reads=[ctx.bufs["x2t"], ctx.bufs["ktm"]],
                  writes=[ctx.bufs["kbf_t"]])
            tpb_, tpbb = ctx.psum_bf[0]
            for c in range(NC):
                em.op("tensor", lambda e, o=tpb_[0:128, c * 128:(c + 1) * 128],
                      i=kbf_t[:, c * 128:(c + 1) * 128], idt=ctx.identb:
                      e.transpose(o, i, idt), reads=[ctx.bufs["kbf_t"], cb],
                      writes=[tpbb])
            em.op("vector", lambda e, o=dst_view,
                  i=tpb_[0:128, 0:1024].rearrange("p (c t) -> p c t", c=NC):
                  e.tensor_copy(o, i), reads=[tpbb], writes=[dst_buf])

        w2b_buf = em.buf("w2bf")
        for rc in range(NOC1):
            g, gb = ctx.tiles["wgA"], ctx.bufs["wgA"]
            em.dma("sync", g[:, 0:1024], w2[rc * 128:(rc + 1) * 128, :], writes=[gb])
            em.op("scalar", lambda e, o=ctx.tiles["wsA"][:, 0:1024], i=g[:, 0:1024]:
                  e.copy(o, i), reads=[gb], writes=[ctx.bufs["wsA"]])
            em.dma("gpsimd", w2bf[rc * 128:(rc + 1) * 128, :], ctx.tiles["wsA"][:, 0:1024],
                   reads=[ctx.bufs["wsA"]], writes=[w2b_buf])
        load_w(wk, wcur, wcb)
        for s in range(20):
            load_xpt(s, HALO0 - 512)
            dv = arena[:, KF0:KF0 + 8 * KT0].rearrange(
                "p (c t) -> p c t", c=NC)[:, :, s * 128:s * 128 + 128]
            rotary_sub(s, t1k, t2k, 128, wcur, wcb, dv, ab_kf)
        load_w(wv, wcur, wcb)
        for s in range(20):
            load_xpt(s, HALO0 - 512)
            for half in range(2):
                pt, pb = ctx.psum[2 + half]
                em.op("tensor", proj_tm(wcur, wcb, half, pt), reads=[wcb, xpb], writes=[pb])
                em.op("vector", lambda e,
                      o=arena[:, VT0 + s * 1024 + half * 512:VT0 + s * 1024 + half * 512 + 512],
                      i=pt[0:128, 0:512]: e.tensor_copy(o, i),
                      reads=[pb], writes=[ab_vt])
        load_w(wq, wcur, wcb)
        load_w(wo, wcur2, wcb2)
        afb = ctx.bufs["afm"]
        qftb = ctx.bufs["qft"]
        for qt in range(17):
            load_xpt(qt, HALO0 - 128)
            dvq = qft[:, :].rearrange("p (c t) -> p c t", c=NC)[:, :, 0:128]
            rotary_sub(qt, t1q, t2q, 0, wcur, wcb, dvq, qftb)
            em.dma("sync", biasA_t[:, :], biasA[qt], writes=[ctx.bufs["biasA_t"]])
            for h in range(16):
                hc, hr = h // 2, (h % 2) * 64
                sp, spb = ctx.psum[2]
                def mms(e, qt=qt, hc=hc, hr=hr, sp=sp):
                    return e.matmul(sp[0:128, 0:384],
                                    qft[hr:hr + 64, hc * 128:(hc + 1) * 128],
                                    arena[hr:hr + 64, KF0 + hc * KT0 + (qt + 1) * 128:
                                          KF0 + hc * KT0 + (qt + 1) * 128 + 384],
                                    start=True, stop=True)
                em.op("tensor", mms, reads=[qftb, ab_kf], writes=[spb])
                em.op("vector", lambda e, o=tmp[:, 0:384], i0=sp[0:128, 0:384],
                      i1=biasA_t[:, :]: e.tensor_add(o, i0, i1),
                      reads=[spb, ctx.bufs["biasA_t"]], writes=[ctx.bufs["tmp"][0]])
                em.op("vector", lambda e, o=mrow[:, 0:1], i=tmp[:, 0:384]:
                      e.reduce_max(o, i, axis=AX.X, negate=True),
                      reads=[ctx.bufs["tmp"][0]], writes=[ctx.bufs["mrow"]])
                em.op("scalar", lambda e, o=psb[:, :], i=tmp[:, 0:384], b=mrow[:, 0:1]:
                      e.activation(o, i, AF.Exp, bias=b),
                      reads=[ctx.bufs["tmp"][0], ctx.bufs["mrow"]], writes=[ctx.bufs["psb"]])
                em.op("vector", lambda e, o=mrow[:, 1:2], i=psb[:, :]:
                      e.reduce_sum(o, i, axis=AX.X), reads=[ctx.bufs["psb"]],
                      writes=[ctx.bufs["mrow"]])
                em.op("vector", lambda e, o=mrow[:, 1:2], i=mrow[:, 1:2]:
                      e.reciprocal(o, i), reads=[ctx.bufs["mrow"]], writes=[ctx.bufs["mrow"]])
                em.op("vector", lambda e, o=psb[:, :], i=psb[:, :], m=mrow[:, 1:2]:
                      e.tensor_single_scalar(o, i, m, ALU.mult),
                      reads=[ctx.bufs["psb"], ctx.bufs["mrow"]], writes=[ctx.bufs["psb"]])
                tb_, tbb = ctx.psum_bf[0]
                for kc2 in range(3):
                    em.op("tensor", lambda e, o=tb_[0:128, kc2 * 128:(kc2 + 1) * 128],
                          i=psb[:, kc2 * 128:(kc2 + 1) * 128], idt=ctx.identb:
                          e.transpose(o, i, idt), reads=[ctx.bufs["psb"], cb], writes=[tbb])
                em.op("vector", lambda e, o=pt_sb[:, :], i=tb_[0:128, 0:384]:
                      e.tensor_copy(o, i), reads=[tbb], writes=[ctx.bufs["pt_sb"]])
                op_, opb = ctx.psum[3]
                def mmo(e, qt=qt, h=h, op_=op_):
                    last = None
                    for kc2 in range(3):
                        last = e.matmul(op_[0:64, 0:128],
                                        arena[:, VT0 + (qt + 1 + kc2) * 1024 + h * 64:
                                              VT0 + (qt + 1 + kc2) * 1024 + h * 64 + 64],
                                        pt_sb[:, kc2 * 128:(kc2 + 1) * 128],
                                        start=(kc2 == 0), stop=(kc2 == 2))
                    return last
                em.op("tensor", mmo, reads=[ctx.bufs["pt_sb"], ab_vt], writes=[opb])
                em.op("scalar", lambda e,
                      o=afm[hr:hr + 64, hc * 128:(hc + 1) * 128],
                      i=op_[0:64, 0:128]: e.copy(o, i), reads=[opb], writes=[afb])
            for c in range(NC):
                em.dma("sync", xrs[:, c * 128:(c + 1) * 128],
                       xin[c][:, (HALO0 - 128) + qt * 128:(HALO0 - 128) + qt * 128 + 128],
                       writes=[ctx.bufs["xrs"]])
            for oc in range(NC):
                pt, pb = ctx.psum[(oc % 2)]
                def mmw(e, oc=oc, pt=pt):
                    last = None
                    for kc in range(NC):
                        last = e.matmul(pt[0:128, 0:128],
                                        wcur2[:, kc * 1024 + oc * 128:kc * 1024 + oc * 128 + 128],
                                        afm[:, kc * 128:(kc + 1) * 128],
                                        start=(kc == 0), stop=(kc == NC - 1))
                    return last
                em.op("tensor", mmw, reads=[wcb2, afb], writes=[pb])
                tpar = oc % 2
                em.op("vector", lambda e, o=tmp[:, tpar * TT:tpar * TT + 128],
                      i0=pt[0:128, 0:128], i1=xrs[:, oc * 128:(oc + 1) * 128]:
                      e.tensor_add(o, i0, i1), reads=[pb, ctx.bufs["xrs"]],
                      writes=[ctx.bufs["tmp"][tpar]])
                if qt == 0:
                    em.op("vector", lambda e, o=tmp[:, tpar * TT:tpar * TT + 128],
                          i0=tmp[:, tpar * TT:tpar * TT + 128], i1=qm0[:, :]:
                          e.tensor_mul(o, i0, i1),
                          reads=[ctx.bufs["qm0"], ctx.bufs["tmp"][tpar]],
                          writes=[ctx.bufs["tmp"][tpar]])
                em.dma("gpsimd", x1dr[oc][:, qt * 128:(qt + 1) * 128],
                       tmp[:, tpar * TT:tpar * TT + 128],
                       reads=[ctx.bufs["tmp"][tpar]], writes=[x1b])

        x1sb = ctx.bufs["x1s"]
        xnb = ctx.bufs["xn_t"]
        x2b = ctx.bufs["x2t"]
        gst = GemmState()
        for tt in range(NTT):
            t0 = tt * TT
            for c in range(NC):
                em.dma("sync", x1s[:, c * 513:(c + 1) * 513],
                       x1dr[c][:, 127 + t0:127 + t0 + 513], reads=[x1b], writes=[x1sb])
            _ln_gb_shift(ctx, x1s, x1sb, xn_t, xnb, lngt, lnbt)
            def ev1(nout0, tt_, pt, pb):
                oc = nout0 // 128
                if oc < NOC1:
                    em.op("scalar", lambda e, o=arena[:, HA0 + oc * TT:HA0 + (oc + 1) * TT],
                          i=pt[0:128, 0:TT], b=b1at[:, oc:oc + 1]:
                          e.activation(o, i, AF.Identity, bias=b),
                          reads=[pb, ctx.bufs["b1at"]], writes=[ab_kf])
                else:
                    oc2 = oc - NOC1
                    tpar = oc2 % 2
                    em.op("scalar", lambda e, o=tmp[:, tpar * TT:(tpar + 1) * TT],
                          i=pt[0:128, 0:TT], b=b1gt[:, oc2:oc2 + 1]:
                          e.activation(o, i, AF.Gelu, bias=b),
                          reads=[pb, ctx.bufs["b1gt"]], writes=[ctx.bufs["tmp"][tpar]])
                    em.op("vector", lambda e, o=arena[:, U0 + oc2 * TT:U0 + (oc2 + 1) * TT],
                          i0=tmp[:, tpar * TT:(tpar + 1) * TT],
                          i1=arena[:, HA0 + oc2 * TT:HA0 + (oc2 + 1) * TT]:
                          e.tensor_mul(o, i0, i1),
                          reads=[ctx.bufs["tmp"][tpar], ab_kf], writes=[ab_u])
            emit_gemm(ctx, x_tile=xn_t, x_buf=xnb, x_col0=0, Kdim=D, w_dram_ap=w1,
                      w_buf=em.buf(), Ndim=2 * NOC1 * 128, evac=ev1, slab_n=256,
                      tts=[0], psum_ids=(0, 1), tile_T=TT, gst=gst, w_f32=True)
            def ev2(nout0, tt_, pt, pb):
                oc = nout0 // 128
                dst = x2t[:, oc * TT:(oc + 1) * TT]
                tpar = oc % 2
                em.op("vector", lambda e, o=tmp[:, tpar * TT:(tpar + 1) * TT],
                      i=pt[0:128, 0:TT], b=b2t[:, oc:oc + 1]:
                      e.tensor_scalar(o, i, b, None, ALU.add),
                      reads=[pb, ctx.bufs["b2t"]], writes=[ctx.bufs["tmp"][tpar]])
                em.op("vector", lambda e, o=dst, i0=tmp[:, tpar * TT:(tpar + 1) * TT],
                      i1=x1s[:, oc * 513 + 1:oc * 513 + 1 + TT]:
                      e.tensor_add(o, i0, i1),
                      reads=[ctx.bufs["tmp"][tpar], x1sb], writes=[x2b])
            emit_gemm(ctx, x_tile=arena, x_buf=ab_u, x_col0=U0, Kdim=NOC1 * 128,
                      w_dram_ap=w2bf, w_buf=w2b_buf, Ndim=D, evac=ev2, slab_n=128,
                      tts=[0], psum_ids=(0, 1), tile_T=TT, gst=gst, w_f32=False)
            if final:
                for c in range(NC):
                    em.dma("sync", sq[:, (c % 2) * TT:(c % 2) * TT + TT],
                           xorig[c][:, t0:t0 + TT], writes=[ctx.bufs["sq"][c % 2]])
                    em.op("vector", lambda e, o=x2t[:, c * TT:(c + 1) * TT],
                          i0=x2t[:, c * TT:(c + 1) * TT],
                          i1=sq[:, (c % 2) * TT:(c % 2) * TT + TT]:
                          e.tensor_add(o, i0, i1),
                          reads=[ctx.bufs["sq"][c % 2], x2b], writes=[x2b])
                emit_ln_tile(ctx, x2t, x2b, yout, t0)
            else:
                for c in range(NC):
                    em.dma("gpsimd", yout[c][:, t0:t0 + TT], x2t[:, c * TT:(c + 1) * TT],
                           reads=[x2b])
        em.finalize()
    return nc


def _ln_gb_shift(ctx, x1s, x1sb, xn_t, xnb, lngt, lnbt, eps=1e-5):
    em = ctx.em
    sq = ctx.tiles["sq"]; sq_bufs = ctx.bufs["sq"]
    row = ctx.tiles["row"]; row_buf = ctx.bufs["row"]
    lb = ctx.ln_banks
    s1 = ctx.psum[lb[0]]; s2 = ctx.psum[lb[1]]; ba = ctx.psum[lb[2]]; bb_ = ctx.psum[lb[3]]
    cb = ctx.cb

    def src(c):
        off = 1 if c < 4 else 0
        return x1s[:, c * 513 + off:c * 513 + off + TT]

    for c in range(NC):
        sqa = sq[:, (c % 2) * TT:(c % 2) * TT + TT]
        em.op("scalar", lambda e, i=src(c), o=sqa: e.activation(o, i, AF.Square),
              reads=[x1sb], writes=[sq_bufs[c % 2]])
        em.op("tensor", lambda e, o=s2[0][0:1, :], r=sqa, st=(c == 0), sp=(c == NC - 1):
              e.matmul(o, ctx.ones_col, r, start=st, stop=sp),
              reads=[sq_bufs[c % 2], cb], writes=[s2[1]])

    def mm_s1(e):
        last = None
        for c in range(NC):
            last = e.matmul(s1[0][0:1, :], ctx.ones_col, src(c),
                            start=(c == 0), stop=(c == NC - 1))
        return last
    em.op("tensor", mm_s1, reads=[x1sb, cb], writes=[s1[1]])
    r_mean = row[0:1, 0 * TT:1 * TT]; r_rstd = row[0:1, 1 * TT:2 * TT]
    r_m2 = row[0:1, 2 * TT:3 * TT]; r_b = row[0:1, 2 * TT:3 * TT]
    em.op("vector", lambda e, o=r_mean, i=s1[0][0:1, :]:
          e.tensor_scalar_mul(o, i, 1.0 / D), reads=[s1[1]], writes=[row_buf])
    em.op("vector", lambda e, o=r_m2, i=r_mean: e.tensor_mul(o, i, i),
          reads=[row_buf], writes=[row_buf])
    em.op("vector", lambda e, o=r_rstd, i0=s2[0][0:1, :], i1=r_m2:
          e.scalar_tensor_tensor(o, i0, 1.0 / D, i1, ALU.mult, ALU.subtract),
          reads=[s2[1], row_buf], writes=[row_buf])
    em.op("vector", lambda e, o=r_rstd, i=r_rstd: e.tensor_scalar_add(o, i, eps),
          reads=[row_buf], writes=[row_buf])
    em.op("scalar", lambda e, o=r_rstd, i=r_rstd: e.activation(o, i, AF.Sqrt),
          reads=[row_buf], writes=[row_buf])
    em.op("vector", lambda e, o=r_rstd, i=r_rstd: e.reciprocal(o, i),
          reads=[row_buf], writes=[row_buf])
    em.op("vector", lambda e, o=r_b, i0=r_mean, i1=r_rstd:
          e.scalar_tensor_tensor(o, i0, -1.0, i1, ALU.mult, ALU.mult),
          reads=[row_buf], writes=[row_buf])
    em.op("tensor", lambda e, o=ba[0][:, :], r=r_rstd:
          e.matmul(o, ctx.ones_row, r, start=True, stop=True),
          reads=[row_buf, cb], writes=[ba[1]])
    em.op("tensor", lambda e, o=bb_[0][:, :], r=r_b:
          e.matmul(o, ctx.ones_row, r, start=True, stop=True),
          reads=[row_buf, cb], writes=[bb_[1]])
    for c in range(NC):
        tmp_ = sq[:, (c % 2) * TT:(c % 2) * TT + TT]
        em.op("vector", lambda e, o=tmp_, i0=src(c), i1=ba[0][:, :]:
              e.tensor_mul(o, i0, i1), reads=[x1sb, ba[1]], writes=[sq_bufs[c % 2]])
        em.op("vector", lambda e, o=tmp_, i0=tmp_, i1=bb_[0][:, :]:
              e.tensor_add(o, i0, i1), reads=[sq_bufs[c % 2], bb_[1]],
              writes=[sq_bufs[c % 2]])
        em.op("vector", lambda e, o=xn_t[:, c * TT:(c + 1) * TT], i=tmp_,
              g=lngt[:, c:c + 1], b_=lnbt[:, c:c + 1]:
              e.tensor_scalar(o, i, g, b_, ALU.mult, ALU.add),
              reads=[sq_bufs[c % 2], ctx.bufs["lngt"], ctx.bufs["lnbt"]],
              writes=[xnb])




def prep_l0_ffn(inputs, x, layer, x_orig=None):
    HALO = HALO0
    DEPTH_IN = 6
    i = layer
    Wqkv = np.asarray(inputs["L0_ffn_Wqkv"], np.float32)[i]
    Wo_ = np.asarray(inputs["L0_ffn_Wo"], np.float32)[i]
    W1_ = np.asarray(inputs["L0_ffn_W1"], np.float32)[i]
    W2_ = np.asarray(inputs["L0_ffn_W2"], np.float32)[i]
    b1_ = np.asarray(inputs["L0_ffn_b1"], np.float32)[i]
    b2_ = np.asarray(inputs["L0_ffn_b2"], np.float32)[i]
    lng_ = np.asarray(inputs["L0_ffn_ln_g"], np.float32)[i]
    lnb_ = np.asarray(inputs["L0_ffn_ln_b"], np.float32)[i]
    qs = np.asarray(inputs["L0_ffn_q_scale"], np.float32)[i]
    ks = np.asarray(inputs["L0_ffn_k_scale"], np.float32)[i]
    w1p = np.zeros((D, 2 * NOC1 * 128), np.float32)
    w1p[:, 0:2730] = W1_[:, 0:2730]
    w1p[:, NOC1 * 128:NOC1 * 128 + 2730] = W1_[:, 2730:5460]
    w2p = np.zeros((NOC1 * 128, D), np.float32)
    w2p[0:2730, :] = W2_
    b1ap = np.zeros(NOC1 * 128, np.float32); b1ap[0:2730] = b1_[0:2730]
    b1gp = np.zeros(NOC1 * 128, np.float32); b1gp[0:2730] = b1_[2730:5460]
    sw = np.r_[np.arange(32, 64), np.arange(0, 32)]
    n = 8192
    half = np.arange(0, 64, 2, dtype=np.float32)
    inv_freq = (1.0 / 10000.0 ** (half / 64.0)).astype(np.float32)
    base_scale = ((half + 0.4 * 64) / (1.4 * 64)).astype(np.float32)
    sgn = np.concatenate([-np.ones(32, np.float32), np.ones(32, np.float32)])

    def tables(pos, inv):
        t = np.maximum(pos, 0).astype(np.float32)
        fr = t[:, None] * inv_freq[None, :]
        fr = np.concatenate([fr, fr], -1)
        power = (t - n // 2) / 128.0
        sc = base_scale[None, :] ** power[:, None]
        sc = np.concatenate([sc, sc], -1).astype(np.float32)
        if inv:
            sc = (1.0 / sc).astype(np.float32)
        t1 = (np.cos(fr) * sc).astype(np.float32)
        t2 = (sgn[None, :] * np.sin(fr) * sc).astype(np.float32)
        return t1, t2

    xp = np.zeros((2, 8192 + HALO, D), np.float32)
    xp[:, HALO:, :] = x
    maps = []
    for core in range(8):
        b, g = core // 4, core % 4
        seg_ = xp[b, g * T: g * T + XT0, :]
        xin = fm(seg_)
        qpos = g * T - 128 + np.arange(QT0)
        kpos = g * T - 512 + np.arange(KT0)
        t1q_, t2q_ = tables(qpos, False)
        t1k_, t2k_ = tables(kpos, True)
        bias = np.zeros((17, 128, 384), np.float32)
        for qt in range(17):
            i_g = g * T - 128 + qt * 128 + np.arange(128)[:, None]
            j_g = g * T - 512 + (qt + 1) * 128 + np.arange(384)[None, :]
            bad = (j_g > i_g) | (j_g < i_g - 256) | (j_g < 0) | (i_g < 0)
            bias[qt][bad] = -3e38
        qm = np.ones((128, 128), np.float32)
        if g == 0:
            qm[:] = 0.0
        m = {
            "xin": xin, "consts": host_consts(),
            "wq": np.ascontiguousarray(Wqkv[:, 0:1024]),
            "wk": np.ascontiguousarray(Wqkv[:, 1024:2048]),
            "wv": np.ascontiguousarray(Wqkv[:, 2048:3072]),
            "wo": np.ascontiguousarray(Wo_),
            "w1": w1p, "w2": w2p,
            "b1a": chunk_vec(b1ap), "b1g": chunk_vec(b1gp),
            "b2": chunk_vec(b2_), "lng": chunk_vec(lng_), "lnb": chunk_vec(lnb_),
            "qs_r": np.ascontiguousarray(np.broadcast_to(qs, (128, 64))),
            "qs_sw": np.ascontiguousarray(np.broadcast_to(qs[sw], (128, 64))),
            "ks_r": np.ascontiguousarray(np.broadcast_to(ks, (128, 64))),
            "ks_sw": np.ascontiguousarray(np.broadcast_to(ks[sw], (128, 64))),
            "t1q": np.ascontiguousarray(t1q_.reshape(17, 128, 64)),
            "t2q": np.ascontiguousarray(t2q_.reshape(17, 128, 64)),
            "t1k": np.ascontiguousarray(t1k_.reshape(20, 128, 64)),
            "t2k": np.ascontiguousarray(t2k_.reshape(20, 128, 64)),
            "biasA": bias, "qmask0": qm,
        }
        if x_orig is not None:
            m["xorig"] = shard_x(x_orig)[core]
        maps.append(m)
    return maps


_L0FFN_NC = {}


def stage_l0_ffn(inputs, x):
    x_orig = x
    xcur = x
    for i in range(6):
        final = (i == 5)
        key = ("final" if final else "mid")
        if key not in _L0FFN_NC:
            _L0FFN_NC[key] = build_l0_ffn(final)
        maps = prep_l0_ffn(inputs, xcur, i, x_orig if final else None)
        res = run_bass_kernel_spmd(_L0FFN_NC[key], maps, core_ids=list(range(8)))
        xcur = unshard_x([np.asarray(r["yout"]) for r in res.results])
    return xcur


def prep_l1_ffn(inputs, x):
    shards = shard_x(x)
    base = {
        "fc1q": np.ascontiguousarray(inputs["L1_ffn_fc1_q"], np.int32),
        "fc1s": np.ascontiguousarray(inputs["L1_ffn_fc1_scales"], np.float32),
        "fc1b": chunk_vec(np.asarray(inputs["L1_ffn_fc1_bias"], np.float32)),
        "fc2q": np.ascontiguousarray(inputs["L1_ffn_fc2_q"], np.int32),
        "fc2s": np.ascontiguousarray(inputs["L1_ffn_fc2_scales"], np.float32),
        "fc2b": chunk_vec(np.asarray(inputs["L1_ffn_fc2_bias"], np.float32)),
        "lng": chunk_vec(np.asarray(inputs["L1_ffn_ln_gamma"], np.float32)),
        "lnb": chunk_vec(np.asarray(inputs["L1_ffn_ln_beta"], np.float32)),
        "consts": host_consts(),
    }
    return [dict(base, xin=shards[i]) for i in range(8)]


def run_stage(build_fn, in_maps, out_names=("yout",)):
    nc = build_fn()
    res = run_bass_kernel_spmd(nc, in_maps, core_ids=list(range(8)))
    return res.results


def stage_l1_ffn(inputs, x):
    res = run_stage(build_l1_ffn, prep_l1_ffn(inputs, x))
    return unshard_x([np.asarray(r["yout"]) for r in res])


def kernel(**inputs):
    x = np.asarray(inputs["L0_attn_x"], np.float32)
    x = stage_l0_attn(inputs, x)
    x = stage_l0_ffn(inputs, x)
    x = stage_l1_attn(inputs, x)
    x = stage_l1_ffn(inputs, x)
    x = stage_l2_attn(inputs, x)
    x = stage_l2_ffn(inputs, x)
    x = stage_l3_attn(inputs, x)
    x = stage_l3_ffn(inputs, x)
    return x
